# version 04
# baseline (speedup 1.0000x reference)
import numpy as np
import ml_dtypes
from contextlib import ExitStack

import concourse.bass as bass
import concourse.mybir as mybir
from concourse.bass_utils import run_bass_kernel_spmd

F32 = mybir.dt.float32
BF16 = mybir.dt.bfloat16
AF = mybir.ActivationFunctionType
ALU = mybir.AluOpType
BFNP = ml_dtypes.bfloat16

NCORES = 8
NSEQ = 2
L = 2048
DM = 2048
NH = 32
NHG = 8
HPG = 4
TC = 512
NTC = L // TC
NKC = DM // 128
CEXP = 12.0
EPS = 1e-5

ENGINES = ("sync", "tensor", "vector", "scalar", "gpsimd")
import os
BUILD_PHASES = os.environ.get("KBUILD", "ABCD")
KD = int(os.environ.get("KD", "5"))


class Ctr:
    __slots__ = ("h", "n", "engine")

    def __init__(self, h, engine=None):
        self.h = h
        self.n = 0
        self.engine = engine


class Ph:
    def __init__(self, nc):
        self.nc = nc
        self.streams = {e: [] for e in ENGINES}
        self.waited = {e: {} for e in ENGINES}

    def wait(self, e, deps):
        todo = []
        for c, v in deps:
            if v <= 0 or c.engine == e:
                continue
            if self.waited[e].get(id(c), 0) >= v:
                continue
            self.waited[e][id(c)] = v
            todo.append((c.h, v))
        if todo:
            def f(eng, todo=todo):
                for h, v in todo:
                    eng.wait_ge(h, v)
            self.streams[e].append(f)

    def op(self, e, fn, inc, k=1, deps=()):
        self.wait(e, deps)
        inc.n += k
        def g(eng, fn=fn, h=inc.h, k=k):
            fn(eng).then_inc(h, k)
        self.streams[e].append(g)
        return inc.n

    def raw(self, e, fn):
        self.streams[e].append(fn)

    def build(self):
        with self.nc.Block() as blk:
            for e in ENGINES:
                fns = self.streams[e]
                if fns:
                    getattr(blk, e)(lambda eng, fns=fns: [f(eng) for f in fns])


def _band_mask(t):
    q0 = t * 128
    m = np.zeros((384, 128), dtype=np.float32)
    m[0:16] = 1.0
    kpos = q0 - 64 + np.arange(320)
    qpos = q0 + np.arange(128)
    band = (np.abs(kpos[:, None] - qpos[None, :]) <= 64) \
        & (kpos[:, None] >= 0) & (kpos[:, None] < L)
    m[64:384] = band.astype(np.float32)
    return m


def host_prep(u, Wqkv, bqkv, Wo, bo, sinks, W0, b0, W1, b1):
    d = {}
    u_t = np.ascontiguousarray(u.transpose(0, 2, 1))
    cols = []
    for hg in range(NHG):
        hs = [HPG * hg + j for j in range(HPG)]
        qA = [64 * h + dd for h in hs for dd in range(32)]
        qB = [64 * h + 32 + dd for h in hs for dd in range(32)]
        kA = [2048 + c for c in qA]
        kB = [2048 + c for c in qB]
        vv = [4096 + 64 * h + dd for h in hs for dd in range(64)]
        cols += qA + qB + kA + kB + vv
    cols = np.array(cols)
    d["wqkv"] = np.ascontiguousarray(Wqkv.T[:, cols])
    d["bqkv_pm"] = np.ascontiguousarray(bqkv[cols].reshape(48, 128).T)
    vrow = np.concatenate([bqkv[cols[hg * 768 + 512: hg * 768 + 768]]
                           for hg in range(NHG)])
    d["bqkv_vrow"] = np.ascontiguousarray(vrow[None, :])
    sa = np.zeros((128, NHG * 64), dtype=np.float32)
    sbv = np.zeros((128, NHG * 64), dtype=np.float32)
    spv = np.zeros((64, NHG * 260), dtype=np.float32)
    for hg in range(NHG):
        for j in range(HPG):
            h = HPG * hg + j
            sa[32 * j:32 * j + 32, hg * 64: hg * 64 + 16] = sinks[h, :, 0:32].T
            sbv[32 * j:32 * j + 32, hg * 64: hg * 64 + 16] = sinks[h, :, 32:64].T
            spv[0:16, hg * 260 + 65 * j: hg * 260 + 65 * j + 64] = sinks[h]
            spv[:, hg * 260 + 65 * j + 64] = 1.0
    d["sinks_a"] = sa
    d["sinks_b"] = sbv
    d["sinks_pv"] = spv
    d["wo_blk"] = np.ascontiguousarray(
        Wo.T.reshape(16, 128, 16, 128).transpose(2, 1, 0, 3)
        .reshape(16, 128, 2048))
    d["w0_blk"] = np.ascontiguousarray(
        W0.T.reshape(16, 128, 64, 128).transpose(2, 1, 0, 3)
        .reshape(64, 128, 2048))
    d["w1_blk"] = np.ascontiguousarray(
        W1.T.reshape(64, 128, 2, 1024).transpose(2, 0, 1, 3))
    d["bo_pm"] = np.ascontiguousarray(bo.reshape(16, 128).T)
    d["b0_pm"] = np.ascontiguousarray(b0.reshape(64, 128).T)
    d["b1_pm"] = np.ascontiguousarray(b1.reshape(16, 128).T)
    half = 32
    inv = 1.0 / (10000.0 ** (np.arange(half, dtype=np.float64) / half))
    ang = np.arange(L, dtype=np.float64)[:, None] * inv[None, :]
    cosT = np.tile(np.cos(ang).T, (4, 1)).astype(np.float32)
    sinT = np.tile(np.sin(ang).T, (4, 1)).astype(np.float32)
    d["rope_cq"] = (0.125 * cosT).astype(BFNP)
    d["rope_sq"] = (0.125 * sinT).astype(BFNP)
    d["rope_ck"] = cosT.astype(BFNP)
    d["rope_sk"] = sinT.astype(BFNP)
    mk = np.zeros((3, 128, 384), dtype=np.float32)
    for vi, t in enumerate([0, 1, 15]):
        m = _band_mask(t)
        for c in range(3):
            mk[vi, :, c * 128:(c + 1) * 128] = m[c * 128:(c + 1) * 128, :]
    d["mask3"] = ((mk - 1.0) * 1e9).astype(BFNP)
    d["id128"] = np.eye(128, dtype=np.float32).astype(BFNP)
    return u_t, d


def build_nc():
    nc = bass.Bass("TRN2", target_bir_lowering=False,
                   detect_race_conditions=(
                       os.environ.get("KRACE", "1") == "1"))

    u_t = nc.dram_tensor("u_t", [NSEQ, DM, L], F32, kind="ExternalInput").ap()
    wqkv = nc.dram_tensor("wqkv", [DM, 6144], F32, kind="ExternalInput").ap()
    bqkv_pm = nc.dram_tensor("bqkv_pm", [128, 48], F32, kind="ExternalInput").ap()
    bqkv_vrow = nc.dram_tensor("bqkv_vrow", [1, 2048], F32, kind="ExternalInput").ap()
    sinks_a = nc.dram_tensor("sinks_a", [128, 512], F32, kind="ExternalInput").ap()
    sinks_b = nc.dram_tensor("sinks_b", [128, 512], F32, kind="ExternalInput").ap()
    sinks_pv = nc.dram_tensor("sinks_pv", [64, 2080], F32, kind="ExternalInput").ap()
    wo_blk = nc.dram_tensor("wo_blk", [16, 128, 2048], F32, kind="ExternalInput").ap()
    w0_blk = nc.dram_tensor("w0_blk", [64, 128, 2048], F32, kind="ExternalInput").ap()
    w1_blk = nc.dram_tensor("w1_blk", [2, 64, 128, 1024], F32, kind="ExternalInput").ap()
    bo_pm = nc.dram_tensor("bo_pm", [128, 16], F32, kind="ExternalInput").ap()
    b0_pm = nc.dram_tensor("b0_pm", [128, 64], F32, kind="ExternalInput").ap()
    b1_pm = nc.dram_tensor("b1_pm", [128, 16], F32, kind="ExternalInput").ap()
    rope_cq = nc.dram_tensor("rope_cq", [128, L], BF16, kind="ExternalInput").ap()
    rope_sq = nc.dram_tensor("rope_sq", [128, L], BF16, kind="ExternalInput").ap()
    rope_ck = nc.dram_tensor("rope_ck", [128, L], BF16, kind="ExternalInput").ap()
    rope_sk = nc.dram_tensor("rope_sk", [128, L], BF16, kind="ExternalInput").ap()
    mask3d = nc.dram_tensor("mask3", [3, 128, 384], BF16, kind="ExternalInput").ap()
    id128d = nc.dram_tensor("id128", [128, 128], BF16, kind="ExternalInput").ap()
    out_t = nc.dram_tensor("out_t", [NSEQ, DM, L], F32, kind="ExternalOutput").ap()
    KDBG = os.environ.get("KDBG", "0") == "1"
    if KDBG:
        dbg_o = nc.dram_tensor("dbg_o", [NSEQ, NH, 64, L], BF16,
                               kind="ExternalOutput").ap()
        dbg_x1 = nc.dram_tensor("dbg_x1", [NSEQ, NKC, 128, L], F32,
                                kind="ExternalOutput").ap()
        dbg_st = nc.dram_tensor("dbg_st", [NSEQ, 4, NTC, 3, TC], F32,
                                kind="ExternalOutput").ap()
        dbg_ns = nc.dram_tensor("dbg_ns", [2, 4096], BF16,
                                kind="ExternalOutput").ap()

    wqkv_scr_h = nc.dram_tensor("wqkv_scr", [8, 128, 12288], BF16, kind="Internal")
    wo_scr_h = nc.dram_tensor("wo_scr", [16, 128, 2048], BF16, kind="Internal")
    w0_scr_h = nc.dram_tensor("w0_scr", [64, 128, 2048], BF16, kind="Internal")
    w1_scr_h = nc.dram_tensor("w1_scr", [2, 64, 128, 1024], BF16, kind="Internal")
    o_scr_h = nc.dram_tensor("o_scr", [NSEQ, NH, 64, L], BF16, kind="Internal")
    x1_scr_h = nc.dram_tensor("x1_scr", [NSEQ, NKC, 128, L], F32, kind="Internal")
    statrow_h = nc.dram_tensor("statrow", [NSEQ, 4, NTC, 3, TC], F32, kind="Internal")
    nsum_h = nc.dram_tensor("nsum_scr", [2, 4096], BF16, kind="Internal")
    rrow_h = nc.dram_tensor("rrow_scr", [NSEQ, NH, 2048], BF16, kind="Internal")

    S = ExitStack()
    with S:
        cT = Ctr(S.enter_context(nc.semaphore("cT")), "tensor")
        cA = Ctr(S.enter_context(nc.semaphore("cA")), "scalar")
        cD = Ctr(S.enter_context(nc.semaphore("cD")), "vector")
        cP = Ctr(S.enter_context(nc.semaphore("cP")), "gpsimd")
        cS = Ctr(S.enter_context(nc.semaphore("cS")), "sync")
        ECTR = {"sync": cS, "tensor": cT, "vector": cD, "scalar": cA,
                "gpsimd": cP}
        DSEMS = {}
        def dsem(name, q):
            c = Ctr(S.enter_context(nc.semaphore(name)))
            c.engine = None
            DSEMS[name] = (c, q)
            return c
        dU = dsem("dU", "sync")
        dSlab = dsem("dSlab", "sync")
        dTbl = dsem("dTbl", "sync")
        dOtc = dsem("dOtc", "sync")
        dWoR = dsem("dWoR", "sync")
        dW0R = dsem("dW0R", "sync")
        dW1R = dsem("dW1R", "sync")
        dX1w = dsem("dX1w", "gpsimd")
        dOw = dsem("dOw", "sync")
        dStat = dsem("dStat", "scalar")
        dNorm = dsem("dNorm", "scalar")
        dPre = dsem("dPre", "scalar")
        dPreX = dsem("dPreX", "scalar")
        dOut = dsem("dOut", "scalar")
        dP0L = dsem("dP0L", "scalar")
        dAcc = dsem("dAcc", "gpsimd")
        dAccX = dsem("dAccX", "gpsimd")
        dP0S = dsem("dP0S", "gpsimd")

        def sbt(name, shape, dt):
            return S.enter_context(nc.sbuf_tensor(name, shape, dt))

        pA = S.enter_context(nc.psum_tensor("pA", [128, 2048], F32))
        pB = S.enter_context(nc.psum_tensor("pB", [128, 2048], F32))

        A1 = sbt("A1", [128, 32768], BF16)
        A2 = sbt("A2", [128, 32768], BF16)
        A3b = sbt("A3", [128, 16384], BF16)
        A2f = A2.bitcast(F32)
        A3f = A3b.bitcast(F32)
        s1evac = A2f[0:1, 15360:16384]
        A1f = A1.bitcast(F32)
        s1evacC = A1f[0:1, 8192:9216]

        bqkv_sb = sbt("bqkvt", [128, 48], F32)
        bo_sb = sbt("bot", [128, 16], F32)
        b0_sb = sbt("b0t", [128, 64], F32)
        b1_sb = sbt("b1t", [128, 16], F32)
        sA_sb = sbt("sAt", [128, 512], BF16)
        sB_sb = sbt("sBt", [128, 512], BF16)
        spv_sb = sbt("spvt", [64, 2080], BF16)
        bvrow_sb = sbt("bvrowt", [1, 2048], BF16)
        eps_t = sbt("epsT", [128, 1], F32)
        cexp_t = sbt("cexpT", [128, 1], F32)
        ones_c = sbt("onesC", [128, 1], BF16)
        ones_r = sbt("onesR", [1, 128], BF16)
        srows = sbt("srows", [1, 1536], F32)
        rc_in = sbt("rcin", [128, 64], BF16)
        rc_out = sbt("rcout", [128, 64], BF16)
        mnr_b = sbt("mnrb", [128, 2 * TC], F32)
        mneg_b = mnr_b[:, 0:TC]
        r_b = mnr_b[:, TC:2 * TC]
        id_sb = sbt("id128t", [128, 128], BF16)
        xsq_sb = sbt("xsq", [128, 1024], BF16)
        p0f = sbt("p0f", [128, 1536], F32)
        p0b = sbt("p0b", [128, 1536], BF16)

        h_sb = A1
        d_sb = A1
        u_sb = A3f
        xbfA = lambda: A2
        SLAB0 = 0
        QA0, QB0, KA0, KB0 = 12288, 14336, 16384, 18432
        ABF0, BBF0, VEXT0, PT0 = 20480, 22528, 24576, 28736
        OPAIR0, RNB0, MSK0 = 0, 4096, 6144
        CQ0, SQ0, CK0, SK0 = 7296, 9344, 11392, 13440
        OTC0, WOR0, XBFC0 = 0, 8192, 16384
        E0, H20, W0R0, W1R0, Y30, X20 = 0, 8192, 16384, 24576, 26624, 28672

        def bar_end(ph, engines):
            toks = {}
            for e in engines:
                c = ECTR[e]
                for nm, (dc, dq) in DSEMS.items():
                    if dq == e:
                        ph.wait(e, [(dc, dc.n)])
                def f(eng, c=c):
                    eng.drain()
                    eng.sem_inc(c.h, 1)
                c.n += 1
                ph.raw(e, f)
                toks[e] = c.n
            return toks

        def bar_start(ph, toks, engines):
            for e in engines:
                ph.wait(e, [(ECTR[e2], toks[e2]) for e2 in toks if e2 != e])

        def dma(ph, sem, out, in_, deps=(), accum=False, serial=False):
            ctr, q = DSEMS[sem] if isinstance(sem, str) else sem
            kw = {"accum_op": ALU.add} if accum else {}
            if serial:
                deps = list(deps) + [(ctr, ctr.n)]
            def f(eng, out=out, in_=in_, kw=kw):
                return eng.dma_start(out=out, in_=in_, **kw)
            return ph.op(q, f, inc=ctr, k=16, deps=deps)

        state = {"statrow_dma": 0, "stats_ev_cd": 0}

        KST = int(os.environ.get("KST", "9"))
        def ln_stats(ph, s, site, tc, xap, chunk_deps, want_mnegr, gate=False,
                     sev=None):
            sev = s1evac if sev is None else sev
            s2_toks = []
            for kc in range(NKC):
                dep = list(chunk_deps(kc))
                if kc == 0:
                    dep.append((cD, state["stats_ev_cd"]))
                if kc >= 2:
                    dep.append((cT, s2_toks[kc - 2]))
                tk = ph.op("vector", lambda eng, kc=kc: eng.tensor_mul(
                    out=xsq_sb[:, (kc % 2) * 512:(kc % 2 + 1) * 512],
                    in0=xap(kc), in1=xap(kc)), inc=cD, deps=dep)
                ph.op("tensor", lambda eng, kc=kc: eng.matmul(
                    pA[0:1, 1024:1536], ones_c[:], xap(kc),
                    start=(kc == 0), stop=(kc == NKC - 1)),
                    inc=cT, deps=dep)
                t2 = ph.op("tensor", lambda eng, kc=kc: eng.matmul(
                    pA[0:1, 1536:2048], ones_c[:],
                    xsq_sb[:, (kc % 2) * 512:(kc % 2 + 1) * 512],
                    start=(kc == 0), stop=(kc == NKC - 1)),
                    inc=cT, deps=[(cD, tk)])
                s2_toks.append(t2)
            if gate and KST < 2:
                return state["statrow_dma"]
            tev = ph.op("vector", lambda eng: eng.tensor_copy(
                out=sev, in_=pA[0:1, 1024:2048]), inc=cD,
                deps=[(cT, s2_toks[-1]), (dStat, state["statrow_dma"])])
            state["stats_ev_cd"] = tev
            t_mneg = ph.op("vector", lambda eng: eng.tensor_scalar_mul(
                out=srows[0:1, 0:512], in0=sev[0:1, 0:512],
                scalar1=-1.0 / DM), inc=cD)
            ph.op("vector", lambda eng: eng.tensor_mul(
                out=srows[0:1, 1024:1536], in0=srows[0:1, 0:512],
                in1=srows[0:1, 0:512]), inc=cD)
            t_v = ph.op("vector", lambda eng: eng.scalar_tensor_tensor(
                out=sev[0:1, 512:1024], in0=sev[0:1, 512:1024],
                scalar=1.0 / DM, in1=srows[0:1, 1024:1536],
                op0=ALU.mult, op1=ALU.subtract), inc=cD)
            if gate and KST < 3:
                return state["statrow_dma"]
            ph.op("scalar", lambda eng: eng.activation(
                out=srows[0:1, 1024:1536], in_=sev[0:1, 512:1024],
                func=AF.Ln, bias=eps_t[0:1, :]), inc=cA, deps=[(cD, t_v)])
            t_r = ph.op("scalar", lambda eng: eng.activation(
                out=srows[0:1, 512:1024], in_=srows[0:1, 1024:1536],
                func=AF.Exp, scale=-0.5), inc=cA)
            deps_w = [(cA, t_r), (cD, t_mneg)]
            if want_mnegr:
                t_mr = ph.op("vector", lambda eng: eng.tensor_mul(
                    out=srows[0:1, 1024:1536], in0=srows[0:1, 0:512],
                    in1=srows[0:1, 512:1024]), inc=cD, deps=[(cA, t_r)])
                deps_w = [(cD, t_mr)]
            if gate and KST < 4:
                return state["statrow_dma"]
            tw = dma(ph, "dStat",
                     statrow_h[s, site, tc].rearrange("a c -> (a c)").unsqueeze(0),
                     srows[0:1, 0:1536], deps=deps_w, serial=True)
            state["statrow_dma"] = tw
            return tw

        def bcast(ph, dst, s, site, tc, kind, deps, n=1):
            src = statrow_h[s, site, tc].rearrange("a c -> (a c)")[
                kind * TC:(kind + n) * TC].unsqueeze(0).broadcast_to(
                [128, n * TC])
            return dma(ph, "dStat", dst, src, deps=deps, serial=True)

        NB = [e for e in ENGINES if e != "gpsimd"]

        ph = Ph(nc)
        for dst, src in [(bqkv_sb[:], bqkv_pm), (bo_sb[:], bo_pm),
                         (b0_sb[:], b0_pm), (b1_sb[:], b1_pm)]:
            dma(ph, "dStat", dst, src, serial=True)
        t1 = dma(ph, "dStat", A2f[:, 0:512], sinks_a, serial=True)
        ph.op("vector", lambda eng: eng.tensor_copy(
            out=sA_sb[:], in_=A2f[:, 0:512]), inc=cD, deps=[(dStat, t1)])
        t2 = dma(ph, "dStat", A2f[:, 512:1024], sinks_b, serial=True)
        ph.op("vector", lambda eng: eng.tensor_copy(
            out=sB_sb[:], in_=A2f[:, 512:1024]), inc=cD, deps=[(dStat, t2)])
        t3 = dma(ph, "dStat", A2f[0:64, 1024:3104], sinks_pv, serial=True)
        ph.op("vector", lambda eng: eng.tensor_copy(
            out=spv_sb[:], in_=A2f[0:64, 1024:3104]), inc=cD, deps=[(dStat, t3)])
        t5 = dma(ph, "dStat", A2f[0:1, 3104:5152], bqkv_vrow, serial=True)
        ph.op("vector", lambda eng: eng.tensor_copy(
            out=bvrow_sb[:], in_=A2f[0:1, 3104:5152]), inc=cD, deps=[(dStat, t5)])
        dma(ph, "dStat", id_sb[:], id128d, serial=True)
        ph.op("vector", lambda eng: eng.memset(eps_t[:], EPS), inc=cD)
        ph.op("vector", lambda eng: eng.memset(cexp_t[:], -CEXP), inc=cD)
        ph.op("vector", lambda eng: eng.memset(ones_c[:], 1.0), inc=cD)
        ph.op("vector", lambda eng: eng.memset(ones_r[:], 1.0), inc=cD)
        prev_toks = bar_end(ph, ENGINES)
        ph.build()

        p0_items = []
        for hg in range(NHG):
            for kc in range(NKC):
                p0_items.append((wqkv[kc * 128:(kc + 1) * 128,
                                      hg * 768:(hg + 1) * 768],
                                 wqkv_scr_h[hg][:, kc * 768:(kc + 1) * 768]))
        for mc in range(16):
            for pc in range(4):
                p0_items.append((wo_blk[mc][:, pc * 512:(pc + 1) * 512],
                                 wo_scr_h[mc][:, pc * 512:(pc + 1) * 512]))
        for mc in range(64):
            for pc in range(4):
                p0_items.append((w0_blk[mc][:, pc * 512:(pc + 1) * 512],
                                 w0_scr_h[mc][:, pc * 512:(pc + 1) * 512]))
        for oh in range(2):
            for kc in range(64):
                for pc in range(2):
                    p0_items.append(
                        (w1_blk[oh, kc][:, pc * 512:(pc + 1) * 512],
                         w1_scr_h[oh, kc][:, pc * 512:(pc + 1) * 512]))
        p0 = {"li": 0, "ci": 0, "store": [0] * len(p0_items),
              "conv": [0] * len(p0_items),
              "ltok": [0] * len(p0_items)}

        def p0_dep_blk(i0):
            return p0["store"][i0]

        def p0_pump(ph, n):
            N = len(p0_items)
            def load_one(i):
                src, dst = p0_items[i]
                o0 = (i % 2) * 768
                w = src.shape[1]
                depl = [(cP, p0["conv"][i - 2])] if i >= 2 else []
                p0["ltok"][i] = dma(ph, "dP0L", p0f[:, o0:o0 + w], src,
                                    deps=depl)
            for _ in range(n):
                while p0["li"] < N and p0["li"] < p0["ci"] + 2:
                    load_one(p0["li"])
                    p0["li"] += 1
                i = p0["ci"]
                if i >= N:
                    return
                p0["ci"] += 1
                src, dst = p0_items[i]
                o0 = (i % 2) * 768
                w = src.shape[1]
                deps_c = [(dP0L, p0["ltok"][i] + (16 if i + 1 < N else 0))]
                if i >= 1:
                    deps_c.append((dP0S, p0["store"][i - 1]))
                tcv = ph.op("gpsimd", lambda eng, o0=o0, w=w:
                            eng.tensor_copy(out=p0b[:, o0:o0 + w],
                                            in_=p0f[:, o0:o0 + w]),
                            inc=cP, deps=deps_c)
                ts = dma(ph, "dP0S", dst, p0b[:, o0:o0 + w])
                p0["conv"][i] = tcv
                p0["store"][i] = ts

        x1stat = {}

        nseq_build = NSEEQ if False else (NSEQ if "D" in BUILD_PHASES or True else 1)
        nseq_build = int(os.environ.get("KSEQ", str(NSEQ)))
        for s in range(nseq_build):
            if "A" not in BUILD_PHASES:
                continue
            ph = Ph(nc)
            bar_start(ph, prev_toks, NB)
            prev_cd = 0
            prev_ct = 0
            for tc in range(NTC):
                ld = []
                for kc in range(NKC):
                    dep = [(cD, prev_cd)] if kc == 0 else []
                    ld.append(dma(ph, "dU", u_sb[:, kc * TC:(kc + 1) * TC],
                                  u_t[s, kc * 128:(kc + 1) * 128,
                                      tc * TC:(tc + 1) * TC], deps=dep))
                cps = []
                for kc in range(NKC):
                    dep = [(dU, ld[-1])]
                    if kc == 0:
                        dep.append((cT, prev_ct))
                    cps.append(ph.op("scalar", lambda eng, kc=kc:
                                     eng.activation(
                            out=A2[:, kc * TC:(kc + 1) * TC],
                            in_=u_sb[:, kc * TC:(kc + 1) * TC], func=AF.Copy),
                            inc=cA, deps=dep))
                tw = ln_stats(ph, s, 0, tc,
                              lambda kc: A2[:, kc * TC:(kc + 1) * TC],
                              lambda kc: [(cA, cps[kc])], False)
                prev_ct = cT.n
                tb1 = bcast(ph, mnr_b[:], s, 0, tc, 0,
                            deps=[(dStat, tw), (cD, cD.n)], n=2)
                tb2 = tb1
                for kc in range(NKC):
                    ph.op("vector", lambda eng, kc=kc: eng.tensor_add(
                        out=u_sb[:, kc * TC:(kc + 1) * TC],
                        in0=u_sb[:, kc * TC:(kc + 1) * TC], in1=mneg_b[:]),
                        inc=cD, deps=[(dStat, tb1)])
                    ph.op("vector", lambda eng, kc=kc, tc=tc: eng.tensor_mul(
                        out=h_sb[:, kc * L + tc * TC: kc * L + (tc + 1) * TC],
                        in0=u_sb[:, kc * TC:(kc + 1) * TC], in1=r_b[:]),
                        inc=cD, deps=[(dStat, tb2)])
                prev_cd = cD.n
                if s == 0:
                    p0_pump(ph, 40)
            prev_toks = bar_end(ph, NB)
            ph.build()

            if "B" not in BUILD_PHASES:
                continue
            ph = Ph(nc)
            bar_start(ph, prev_toks, NB)
            tbl_toks = []
            for off, src in [(CQ0, rope_cq), (SQ0, rope_sq),
                             (CK0, rope_ck), (SK0, rope_sk)]:
                tbl_toks.append(dma(ph, "dTbl", A3b[:, off:off + 2048], src))
            tmsk = dma(ph, "dTbl",
                       A3b[:, MSK0:MSK0 + 1152].rearrange(
                           "p (v c) -> p v c", v=3),
                       mask3d.transpose([1, 0, 2]))
            tbl_all = tmsk
            slab_cons = 0
            exp_tok = [0, 0]
            pv_tok = [0, 0]
            oev_tok = [0, 0]
            att_done_ct = 0
            last_oscr = 0
            opair = A3b[0:65, OPAIR0:OPAIR0 + 4096]

            def tabap(off):
                return A3b[:, off:off + 2048]

            for hg in range(NHG):
                depl = [(cT, slab_cons)]
                if s == 0:
                    depl.append((dP0S, p0["store"][hg * 16 + 15] + 16))
                tslab = dma(ph, "dSlab", A2[:, SLAB0:SLAB0 + 12288],
                            wqkv_scr_h[hg], deps=depl)
                rope_final = {}
                rope_abf = {}
                evk = {}
                for mi, (dA0, dB0, tC0, tS0) in enumerate(
                        [(QA0, QB0, CQ0, SQ0), (KA0, KB0, CK0, SK0)]):
                    dsta = A2[:, dA0:dA0 + 2048]
                    dstb = A2[:, dB0:dB0 + 2048]
                    tabC = tabap(tC0)
                    tabS = tabap(tS0)
                    abf = A2[:, ABF0:ABF0 + 2048]
                    bbf = A2[:, BBF0:BBF0 + 2048]
                    mm_last = {}
                    for half in range(2):
                        mci = mi * 2 + half
                        ps = pA if half == 0 else pB
                        for tcq in range(NTC):
                            for kc in range(NKC):
                                dep = [(dSlab, tslab)]
                                if tcq == 0 and kc == 0:
                                    if mi == 0:
                                        dep += [(cA, exp_tok[0]),
                                                (cA, exp_tok[1]),
                                                (cD, oev_tok[0]),
                                                (cD, oev_tok[1]),
                                                (cT, pv_tok[0]),
                                                (cT, pv_tok[1])]
                                    else:
                                        dep += [(cA, evk[(0, half)]),
                                                (cD, rope_abf[0])]
                                ph.op("tensor", lambda eng, ps=ps, tcq=tcq,
                                      kc=kc, mci=mci: eng.matmul(
                                        ps[:, tcq * 512:(tcq + 1) * 512],
                                        A2[:, SLAB0 + kc * 768 + mci * 128:
                                           SLAB0 + kc * 768 + (mci + 1) * 128],
                                        h_sb[:, kc * L + tcq * TC:
                                             kc * L + (tcq + 1) * TC],
                                        start=(kc == 0), stop=(kc == NKC - 1)),
                                      inc=cT, deps=dep)
                        mm_last[half] = cT.n
                    bca = bqkv_sb[:, hg * 6 + mi * 2: hg * 6 + mi * 2 + 1]
                    bcb = bqkv_sb[:, hg * 6 + mi * 2 + 1: hg * 6 + mi * 2 + 2]
                    depe = [(cT, mm_last[0])]
                    depe2 = [(cT, mm_last[1])]
                    if mi == 1:
                        depe.append((cD, rope_abf[0]))
                        depe2.append((cD, rope_abf[0]))
                    te1 = ph.op("scalar", lambda eng, bca=bca, abf=abf:
                                eng.activation(out=abf, in_=pA[:],
                                               func=AF.Identity, bias=bca),
                                inc=cA, deps=depe)
                    te2 = ph.op("scalar", lambda eng, bcb=bcb, bbf=bbf:
                                eng.activation(out=bbf, in_=pB[:],
                                               func=AF.Identity, bias=bcb),
                                inc=cA, deps=depe2)
                    evk[(mi, 0)] = te1
                    evk[(mi, 1)] = te2
                    d1 = [(cA, te1), (cA, te2), (dTbl, tbl_all)]
                    ph.op("vector", lambda eng, dsta=dsta, abf=abf, tabC=tabC:
                          eng.tensor_mul(out=dsta, in0=abf, in1=tabC),
                          inc=cD, deps=d1)
                    ph.op("vector", lambda eng, dstb=dstb, bbf=bbf, tabS=tabS:
                          eng.tensor_mul(out=dstb, in0=bbf, in1=tabS), inc=cD)
                    ph.op("vector", lambda eng, dsta=dsta, dstb=dstb:
                          eng.tensor_sub(out=dsta, in0=dsta, in1=dstb), inc=cD)
                    ta4 = ph.op("vector", lambda eng, abf=abf, tabS=tabS:
                                eng.tensor_mul(out=abf, in0=abf, in1=tabS),
                                inc=cD)
                    ph.op("vector", lambda eng, dstb=dstb, bbf=bbf, tabC=tabC:
                          eng.tensor_mul(out=dstb, in0=bbf, in1=tabC), inc=cD)
                    trp = ph.op("vector", lambda eng, dstb=dstb, abf=abf:
                                eng.tensor_add(out=dstb, in0=abf, in1=dstb),
                                inc=cD)
                    rope_abf[mi] = trp
                    rope_final[mi] = trp
                vext0 = VEXT0
                tms = ph.op("vector", lambda eng: eng.memset(
                    A2[:, vext0:vext0 + 4160], 1.0), inc=cD,
                    deps=[(cT, att_done_ct)])
                vevac = []
                for tb in range(16):
                    slot = tb % 8
                    ps = pA if slot < 4 else pB
                    col = (slot % 4) * 512
                    dep0 = [(cA, evk[(1, 0)]), (cA, evk[(1, 1)]),
                            (cD, rope_abf[1])]
                    if tb >= 8:
                        dep0.append((cA, vevac[tb - 8]))
                    ph.op("tensor", lambda eng, ps=ps, col=col, hg=hg:
                          eng.matmul(ps[:, col:col + 256], ones_r[:],
                                     bvrow_sb[0:1, hg * 256:(hg + 1) * 256],
                                     start=True, stop=False),
                          inc=cT, deps=dep0)
                    tbm = 0
                    for kc in range(NKC):
                        tbm = ph.op("tensor", lambda eng, ps=ps, col=col,
                                    kc=kc, tb=tb: eng.matmul(
                                ps[:, col:col + 256],
                                h_sb[:, kc * L + tb * 128:
                                     kc * L + tb * 128 + 128],
                                A2[:, SLAB0 + kc * 768 + 512:
                                   SLAB0 + kc * 768 + 768],
                                start=False, stop=(kc == NKC - 1)),
                            inc=cT, deps=[(dSlab, tslab)])
                    tve = ph.op("scalar", lambda eng, ps=ps, col=col, tb=tb:
                                eng.activation(
                            out=A2[:, vext0 + tb * 260:
                                   vext0 + tb * 260 + 260].rearrange(
                                "p (j dd) -> p j dd", j=4)[:, :, 0:64],
                            in_=ps[:, col:col + 256].rearrange(
                                "p (j dd) -> p j dd", j=4),
                            func=AF.Copy), inc=cA, deps=[(cT, tbm), (cD, tms)])
                    vevac.append(tve)
                slab_cons = cT.n
                for pp in range(2):
                    for t in range(16):
                        slot = t % 2
                        sbase = slot * 1024
                        q0 = t * 128
                        mt = 0 if t == 0 else (2 if t == 15 else 1)
                        dep_first = [(cD, rope_final[0]), (cD, rope_final[1]),
                                     (cA, exp_tok[slot]), (cA, vevac[15])]
                        first = True
                        for jj in range(2):
                            j = pp * 2 + jj
                            b = 32 * j
                            base = sbase + jj * 512
                            w0s = q0 - 64 if t > 0 else 0
                            w2s = q0 + 128 if t < 15 else q0
                            ka = A2[:, KA0:KA0 + 2048]
                            kb = A2[:, KB0:KB0 + 2048]
                            groups = [
                                (sA_sb[b:b + 32, hg * 64: hg * 64 + 64],
                                 sB_sb[b:b + 32, hg * 64: hg * 64 + 64],
                                 pA[0:64, base:base + 128], (b, 0)),
                                (ka[b:b + 32, w0s:w0s + 64],
                                 kb[b:b + 32, w0s:w0s + 64],
                                 pA[64:128, base:base + 128], (b, 64)),
                                (ka[b:b + 32, q0:q0 + 128],
                                 kb[b:b + 32, q0:q0 + 128],
                                 pA[0:128, base + 128:base + 256], (b, 0)),
                                (ka[b:b + 32, w2s:w2s + 128],
                                 kb[b:b + 32, w2s:w2s + 128],
                                 pA[0:128, base + 256:base + 384], (b, 0)),
                            ]
                            qaap = A2[b:b + 32, QA0 + q0:QA0 + q0 + 128]
                            qbap = A2[b:b + 32, QB0 + q0:QB0 + q0 + 128]
                            mregs = ((0, 64, 0), (64, 128, 0),
                                     (0, 128, 128), (0, 128, 256))
                            for gi, (lA, lB, outap, tp) in enumerate(groups):
                                mp0, mp1, rc = mregs[gi]
                                mc = MSK0 + mt * 384 + rc
                                ph.op("tensor", lambda eng, base=base, mc=mc,
                                      mp0=mp0, mp1=mp1, rc=rc: eng.matmul(
                                        pA[mp0:mp1, base + rc:base + rc + 128],
                                        id_sb[mp0:mp1, mp0:mp1],
                                        A3b[mp0:mp1, mc:mc + 128],
                                        start=True, stop=False,
                                        tile_position=(mp0, mp0)),
                                      inc=cT,
                                      deps=(dep_first + [(dTbl, tbl_all)])
                                      if first else [])
                                first = False
                                ph.op("tensor", lambda eng, outap=outap, lA=lA,
                                      qaap=qaap, tp=tp: eng.matmul(
                                        outap, lA, qaap, start=False,
                                        stop=False, tile_position=tp),
                                      inc=cT)
                                ph.op("tensor", lambda eng, outap=outap, lB=lB,
                                      qbap=qbap, tp=tp: eng.matmul(
                                        outap, lB, qbap, start=False,
                                        stop=True, tile_position=tp), inc=cT)
                        tmm_last = cT.n
                        texp = ph.op("scalar", lambda eng, sbase=sbase,
                                     slot=slot: eng.activation(
                                out=A2[:, PT0 + slot * 768:
                                       PT0 + (slot + 1) * 768].rearrange(
                                    "p (hh c) -> p hh c", hh=2),
                                in_=pA[:, sbase:sbase + 1024].rearrange(
                                    "p (hh c) -> p hh c", hh=2)[:, :, 0:384],
                                func=AF.Exp, bias=cexp_t[:]), inc=cA,
                                deps=[(cT, tmm_last), (cT, pv_tok[slot])])
                        exp_tok[slot] = texp
                        obank = (t // 2) % 2
                        ocol = obank * 512 + (t % 2) * 256
                        blkm = max(t - 1, 0)
                        blkp = min(t + 1, 15)
                        for jj in range(2):
                            j = pp * 2 + jj
                            oreg = pB[0:65, ocol + jj * 128:
                                      ocol + jj * 128 + 128]
                            ptb = PT0 + slot * 768 + 384 * jj
                            dep_pv = [(cA, texp), (cD, oev_tok[obank]),
                                      (cA, vevac[blkp]), (cA, vevac[blkm]),
                                      (cA, vevac[t])]
                            ph.op("tensor", lambda eng, oreg=oreg, hg=hg, j=j,
                                  ptb=ptb: eng.matmul(
                                    oreg,
                                    spv_sb[:, hg * 260 + 65 * j:
                                           hg * 260 + 65 * j + 65],
                                    A2[0:64, ptb:ptb + 128],
                                    start=True, stop=False,
                                    tile_position=(0, 0)), inc=cT,
                                  deps=dep_pv)
                            ph.op("tensor", lambda eng, oreg=oreg, t=t, j=j,
                                  ptb=ptb, vext0=vext0: eng.matmul(
                                    oreg,
                                    A2[:, vext0 + t * 260 + 65 * j:
                                       vext0 + t * 260 + 65 * j + 65],
                                    A2[:, ptb + 128:ptb + 256],
                                    start=False, stop=False), inc=cT)
                            ph.op("tensor", lambda eng, oreg=oreg, blkm=blkm,
                                  j=j, ptb=ptb, vext0=vext0: eng.matmul(
                                    oreg,
                                    A2[64:128, vext0 + blkm * 260 + 65 * j:
                                       vext0 + blkm * 260 + 65 * j + 65],
                                    A2[64:128, ptb:ptb + 128],
                                    start=False, stop=False,
                                    tile_position=(64, 0)), inc=cT)
                            pv_tok[slot] = ph.op(
                                "tensor", lambda eng, oreg=oreg, blkp=blkp,
                                j=j, ptb=ptb, vext0=vext0: eng.matmul(
                                    oreg,
                                    A2[:, vext0 + blkp * 260 + 65 * j:
                                       vext0 + blkp * 260 + 65 * j + 65],
                                    A2[:, ptb + 256:ptb + 384],
                                    start=False, stop=True), inc=cT)
                        if t % 2 == 1:
                            t0 = t - 1
                            depv = [(cT, pv_tok[slot]), (dOw, last_oscr)]
                            tev = ph.op(
                                "vector", lambda eng, obank=obank, t0=t0:
                                eng.tensor_copy(
                                    out=opair.rearrange(
                                        "p (j c) -> p j c", j=2)[
                                        :, :, t0 * 128:t0 * 128 + 256]
                                    .rearrange(
                                        "p j (tt c) -> p j tt c", tt=2),
                                    in_=pB[0:65, obank * 512:
                                           (obank + 1) * 512].rearrange(
                                        "p (tt j c) -> p j tt c",
                                        tt=2, j=2)), inc=cD, deps=depv)
                            oev_tok[obank] = tev
                    for jj in range(2):
                        j = pp * 2 + jj
                        last_oscr = dma(ph, "dOw",
                                        o_scr_h[s, hg * 4 + j],
                                        opair[0:64, jj * 2048:(jj + 1) * 2048],
                                        deps=[(cD, oev_tok[0]),
                                              (cD, oev_tok[1])])
                    tn1 = dma(ph, "dNorm", nsum_h[pp].unsqueeze(0),
                              opair[64:65, 0:4096],
                              deps=[(cD, oev_tok[0]), (cD, oev_tok[1])],
                              serial=True)
                    tn2 = dma(ph, "dNorm", rc_in[:, 0:32],
                              nsum_h[pp].rearrange("(p c) -> p c", p=128),
                              deps=[(dNorm, tn1), (cD, cD.n)], serial=True)
                    def _recip(eng):
                        with nc.allow_low_precision(
                                reason="softmax denominators kept bf16"):
                            return eng.reciprocal(out=rc_out[:, 0:32],
                                                  in_=rc_in[:, 0:32])
                    trc = ph.op("vector", _recip, inc=cD,
                                deps=[(dNorm, tn2)])
                    tn3 = dma(ph, "dNorm",
                              rrow_h[s, hg * 4 + pp * 2:hg * 4 + pp * 2 + 2]
                              .rearrange("h t -> (h t)")
                              .rearrange("(p c) -> p c", p=128),
                              rc_out[:, 0:32], deps=[(cD, trc)], serial=True)
                att_done_ct = cT.n
                if s == 0:
                    p0_pump(ph, 70)
            if s == 0:
                p0_pump(ph, len(p0_items))
            prev_toks = bar_end(ph, NB)
            ph.build()

            if "C" not in BUILD_PHASES:
                continue
            ph = Ph(nc)
            bar_start(ph, prev_toks, NB)
            ph.wait("gpsimd", [(ECTR[e], prev_toks[e]) for e in prev_toks])
            wo_mm_prev = 0
            prev_trb = 0
            wo_last_mm = [0] * 16
            for tc in range(NTC):
                o_toks = []
                for hh in range(NH):
                    dep = [(cT, wo_mm_prev)] if hh == 0 else []
                    o_toks.append(dma(
                        ph, "dOtc",
                        A2[(hh % 2) * 64:(hh % 2) * 64 + 64,
                           OTC0 + (hh // 2) * TC: OTC0 + (hh // 2 + 1) * TC],
                        o_scr_h[s, hh][:, tc * TC:(tc + 1) * TC], deps=dep))
                RB0c = 24576
                for hh in range(NH):
                    o_toks.append(dma(
                        ph, "dOtc",
                        A2[(hh % 2) * 64:(hh % 2) * 64 + 64,
                           RB0c + (hh // 2) * TC: RB0c + (hh // 2 + 1) * TC],
                        rrow_h[s, hh][tc * TC:(tc + 1) * TC].unsqueeze(0)
                        .broadcast_to([64, TC]),
                        deps=[(cD, prev_trb)] if hh == 0 else []))
                o_all = o_toks[-1]
                prev_trb = ph.op("vector", lambda eng: eng.tensor_mul(
                    out=A2[:, OTC0:OTC0 + 8192],
                    in0=A2[:, OTC0:OTC0 + 8192],
                    in1=A2[:, RB0c:RB0c + 8192]), inc=cD,
                    deps=[(dOtc, o_all)])
                ev_toks = []
                mm_toks = []
                wol = []
                for mc in range(16):
                    wbuf = mc % 4
                    dep_w = [(cT, mm_toks[mc - 2])] if mc >= 2 else \
                        [(cT, wo_last_mm[14 + mc])]
                    if s == 0 and tc == 0 and mc < 2:
                        dep_w.append((dP0S, 16 * len(p0_items)))
                    twl = dma(ph, "dWoR",
                              A2[:, WOR0 + wbuf * 2048:
                                 WOR0 + (wbuf + 1) * 2048],
                              wo_scr_h[mc], deps=dep_w)
                    wol.append(twl)
                    ps = pA if mc % 2 == 0 else pB
                    tmm = 0
                    wtok = twl + 16
                    for kc in range(NKC):
                        dep = [(dWoR, wtok), (cD, prev_trb)]
                        if mc >= 2 and kc == 0:
                            dep.append((cA, ev_toks[mc - 2]))
                        tmm = ph.op("tensor", lambda eng, ps=ps, wbuf=wbuf,
                                    kc=kc: eng.matmul(
                                ps[:, 0:512],
                                A2[:, WOR0 + wbuf * 2048 + kc * 128:
                                   WOR0 + wbuf * 2048 + (kc + 1) * 128],
                                A2[:, OTC0 + kc * TC: OTC0 + (kc + 1) * TC],
                                start=(kc == 0), stop=(kc == NKC - 1)),
                                inc=cT, deps=dep)
                    mm_toks.append(tmm)
                    ev_toks.append(ph.op("scalar", lambda eng, ps=ps, mc=mc:
                                         eng.activation(
                            out=A2[:, XBFC0 + mc * TC: XBFC0 + (mc + 1) * TC],
                            in_=ps[:, 0:512], func=AF.Identity,
                            bias=bo_sb[:, mc:mc + 1]), inc=cA,
                            deps=[(cT, tmm)]))
                wo_mm_prev = cT.n
                wo_last_mm = mm_toks
                dma(ph, "dWoR", p0b[0:1, 0:64], wo_scr_h[0][0:1, 0:64],
                    deps=[(cT, mm_toks[12])])
                tw = ln_stats(ph, s, 1, tc,
                              lambda kc: A2[:, XBFC0 + kc * TC:
                                            XBFC0 + (kc + 1) * TC],
                              lambda kc: [(cA, ev_toks[kc])], False,
                              sev=s1evacC)
                tb1 = bcast(ph, mnr_b[:], s, 1, tc, 0,
                            deps=[(dStat, tw), (cD, cD.n)], n=2)
                tb2 = tb1
                x1a = []
                for kc in range(NKC):
                    dep1 = [(dStat, tb1)]
                    if kc == 0:
                        dep1 += [(dAcc, dAcc.n), (dX1w, dX1w.n)]
                    ph.op("vector", lambda eng, kc=kc: eng.tensor_add(
                        out=u_sb[:, kc * TC:(kc + 1) * TC],
                        in0=A2[:, XBFC0 + kc * TC: XBFC0 + (kc + 1) * TC],
                        in1=mneg_b[:]), inc=cD, deps=dep1)
                    t2 = ph.op("vector", lambda eng, kc=kc: eng.tensor_mul(
                        out=u_sb[:, kc * TC:(kc + 1) * TC],
                        in0=u_sb[:, kc * TC:(kc + 1) * TC], in1=r_b[:]),
                        inc=cD, deps=[(dStat, tb2)])
                    x1a.append(dma(ph, "dAcc",
                                   u_sb[:, kc * TC:(kc + 1) * TC],
                                   u_t[s, kc * 128:(kc + 1) * 128,
                                       tc * TC:(tc + 1) * TC],
                                   deps=[(cD, t2)], accum=True))
                cp2 = []
                for kc in range(NKC):
                    cp2.append(ph.op("scalar", lambda eng, kc=kc:
                                     eng.activation(
                            out=A1[:, kc * TC:(kc + 1) * TC],
                            in_=u_sb[:, kc * TC:(kc + 1) * TC], func=AF.Copy),
                            inc=cA, deps=[(dAcc, x1a[-1])]))
                    dma(ph, "dX1w",
                        x1_scr_h[s, kc][:, tc * TC:(tc + 1) * TC],
                        u_sb[:, kc * TC:(kc + 1) * TC],
                        deps=[(dAcc, x1a[-1])])
                x1stat[(s, tc)] = ln_stats(
                    ph, s, 2, tc,
                    lambda kc: A1[:, kc * TC:(kc + 1) * TC],
                    lambda kc: [(cA, cp2[kc])], False, sev=s1evacC)
            prev_toks = bar_end(ph, ENGINES)
            ph.build()

            if "D" not in BUILD_PHASES:
                continue
            ph = Ph(nc)
            bar_start(ph, prev_toks, ENGINES)
            h2_prev = 0
            e_prev = 0
            x2_last_cd = 0
            w0_last_mm = [0] * 64
            w1_last_pair = [0, 0]
            w0_final = 0
            for tc in range(NTC):
                tb2 = bcast(ph, r_b, s, 2, tc, 1,
                            deps=[(dStat, x1stat[(s, tc)]), (cD, cD.n)])
                pre_toks = []
                for kc in range(NKC):
                    dep = [(cD, h2_prev)] if kc == 0 else []
                    pre_toks.append(dma(
                        ph, "dPre", u_sb[:, kc * TC:(kc + 1) * TC],
                        statrow_h[s, 2, tc, 0].unsqueeze(0)
                        .broadcast_to([128, TC]), deps=dep))
                acc_toks = []
                for kc in range(NKC):
                    acc_toks.append(dma(
                        ph, "dAcc", u_sb[:, kc * TC:(kc + 1) * TC],
                        x1_scr_h[s, kc][:, tc * TC:(tc + 1) * TC],
                        deps=[(dPre, pre_toks[-1])], accum=True))
                h2_toks = []
                for kc in range(NKC):
                    h2_toks.append(ph.op("vector", lambda eng, kc=kc:
                                         eng.tensor_mul(
                            out=A2[:, H20 + kc * TC: H20 + (kc + 1) * TC],
                            in0=u_sb[:, kc * TC:(kc + 1) * TC], in1=r_b[:]),
                            inc=cD, deps=[(dAcc, acc_toks[-1]),
                                          (dStat, tb2), (cT, w0_final)]))
                h2_prev = h2_toks[-1]
                g_toks = []
                w0mm = []
                for mc in range(64):
                    wbuf = mc % 4
                    dep_w = [(cT, w0mm[mc - 2])] if mc >= 2 else \
                        [(cT, w0_last_mm[62 + mc])]
                    if s == 0 and tc == 0 and mc < 2:
                        dep_w.append((dP0S, 16 * len(p0_items)))
                    twl = dma(ph, "dW0R",
                              A2[:, W0R0 + wbuf * 2048:
                                 W0R0 + (wbuf + 1) * 2048],
                              w0_scr_h[mc], deps=dep_w)
                    ps = pA if mc % 2 == 0 else pB
                    pcol = ((mc % 4) // 2) * 512
                    tmm = 0
                    wtok = twl + 16
                    for kc in range(NKC):
                        dep = [(dW0R, wtok), (cD, h2_toks[kc])]
                        if kc == 0:
                            dep.append((cA, g_toks[mc - 2] if mc >= 2
                                        else e_prev))
                        tmm = ph.op("tensor", lambda eng, ps=ps, pcol=pcol,
                                    wbuf=wbuf, kc=kc: eng.matmul(
                                ps[:, pcol:pcol + 512],
                                A2[:, W0R0 + wbuf * 2048 + kc * 128:
                                   W0R0 + wbuf * 2048 + (kc + 1) * 128],
                                A2[:, H20 + kc * TC: H20 + (kc + 1) * TC],
                                start=(kc == 0), stop=(kc == NKC - 1)),
                                inc=cT, deps=dep)
                    w0mm.append(tmm)
                    g_toks.append(ph.op("scalar", lambda eng, ps=ps, pcol=pcol,
                                        mc=mc: eng.activation(
                            out=d_sb[:, mc * TC:(mc + 1) * TC],
                            in_=ps[:, pcol:pcol + 512], func=AF.Gelu,
                            bias=b0_sb[:, mc:mc + 1]), inc=cA,
                            deps=[(cT, tmm)]))
                w0_last_mm = w0mm
                w0_final = w0mm[-1]
                dma(ph, "dW0R", p0b[0:1, 128:192], w0_scr_h[0][0:1, 0:64],
                    deps=[(cT, w0mm[60])])
                e_ev = []
                for oh in range(2):
                    mm_last = []
                    for kc in range(64):
                        wbuf = kc % 2
                        dep_w = [(cT, mm_last[kc - 2])] if kc >= 2 else \
                            [(cT, w1_last_pair[kc % 2])]
                        if s == 0 and tc == 0 and oh == 0 and kc < 2:
                            dep_w.append((dP0S, 16 * len(p0_items)))
                        twl = dma(ph, "dW1R",
                                  A2[:, W1R0 + wbuf * 1024:
                                     W1R0 + (wbuf + 1) * 1024],
                                  w1_scr_h[oh, kc], deps=dep_w)
                        tmm = 0
                        wtok = twl + 16
                        for oc in range(8):
                            ps = pA if oc < 4 else pB
                            pcol = (oc % 4) * 512
                            dep = [(dW1R, wtok), (cA, g_toks[kc])]
                            if kc == 0:
                                dep.append((cA, g_toks[-1] if oh == 0
                                            else e_ev[-1]))
                                dep.append((cD, state["stats_ev_cd"]))
                            tmm = ph.op("tensor", lambda eng, ps=ps, pcol=pcol,
                                        wbuf=wbuf, oc=oc, kc=kc: eng.matmul(
                                    ps[:, pcol:pcol + 512],
                                    A2[:, W1R0 + wbuf * 1024 + oc * 128:
                                       W1R0 + wbuf * 1024 + (oc + 1) * 128],
                                    d_sb[:, kc * TC:(kc + 1) * TC],
                                    start=(kc == 0), stop=(kc == 63)),
                                    inc=cT, deps=dep)
                        mm_last.append(tmm)
                        w1_last_pair[kc % 2] = tmm
                    for oc in range(8):
                        ps = pA if oc < 4 else pB
                        pcol = (oc % 4) * 512
                        idx = oh * 8 + oc
                        e_ev.append(ph.op("scalar", lambda eng, ps=ps,
                                          pcol=pcol, idx=idx: eng.activation(
                                out=A2[:, E0 + idx * TC: E0 + (idx + 1) * TC],
                                in_=ps[:, pcol:pcol + 512], func=AF.Identity,
                                bias=b1_sb[:, idx:idx + 1]), inc=cA,
                                deps=[(cT, mm_last[-1]),
                                      (cD, x2_last_cd)]))
                dma(ph, "dW1R", p0b[0:1, 64:128], w1_scr_h[0, 0][0:1, 0:64],
                    deps=[(cT, w1_last_pair[0])])
                tw3 = ln_stats(ph, s, 3, tc,
                               lambda kc: A2[:, E0 + kc * TC:
                                             E0 + (kc + 1) * TC],
                               lambda kc: [(cA, e_ev[-1] if kc == 0
                                            else e_ev[kc])], True)
                tb31 = bcast(ph, r_b, s, 3, tc, 1,
                             deps=[(dStat, tw3), (cD, cD.n)])
                y3 = lambda par: A2f[:, (Y30 // 2) + par * 512:
                                     (Y30 // 2) + (par + 1) * 512]
                x2t = lambda par: A2f[:, (X20 // 2) + par * 512:
                                      (X20 // 2) + (par + 1) * 512]
                x2tok = [0, 0]
                for kc in range(NKC):
                    par = kc % 2
                    dep_z = [(dStat, tb31)]
                    if kc >= 2:
                        dep_z.append((cD, x2tok[par]))
                    tpz = dma(ph, "dPreX", x2t(par),
                              statrow_h[s, 3, tc, 2].unsqueeze(0)
                              .broadcast_to([128, TC]), deps=dep_z,
                              serial=True)
                    taz = dma(ph, "dAccX", x2t(par),
                              x1_scr_h[s, kc][:, tc * TC:(tc + 1) * TC],
                              deps=[(dPreX, tpz)], accum=True, serial=True)
                    ph.op("vector", lambda eng, kc=kc, par=par:
                          eng.tensor_mul(
                            out=y3(par),
                            in0=A2[:, E0 + kc * TC: E0 + (kc + 1) * TC],
                            in1=r_b[:]), inc=cD, deps=[(dStat, tb31)])
                    tx = ph.op("vector", lambda eng, par=par:
                               eng.tensor_add(
                            out=x2t(par), in0=x2t(par), in1=y3(par)),
                            inc=cD, deps=[(dAccX, taz)])
                    x2tok[par] = tx
                    dma(ph, "dOut",
                        out_t[s, kc * 128:(kc + 1) * 128,
                              tc * TC:(tc + 1) * TC],
                        x2t(par), deps=[(cD, tx)], serial=True)
                e_prev = e_ev[-1]
                x2_last_cd = cD.n
            prev_toks = bar_end(ph, ENGINES)
            ph.build()

        ph = Ph(nc)
        bar_start(ph, prev_toks, ENGINES)
        if KDBG:
            dma(ph, "dTbl", dbg_o, o_scr_h.ap())
            dma(ph, "dTbl", dbg_x1, x1_scr_h.ap())
            dma(ph, "dTbl", dbg_st, statrow_h.ap())
            dma(ph, "dTbl", dbg_ns, nsum_h.ap())
        def _final_waits(eng):
            for nm, (dc, dq) in DSEMS.items():
                eng.wait_ge(dc.h, dc.n)
        ph.raw("sync", _final_waits)
        ph.build()

    return nc


def kernel(L0_attn_u, L0_attn_Wqkv, L0_attn_bqkv, L0_attn_Wo, L0_attn_bo,
           L0_attn_sinks, L0_ffn_W0, L0_ffn_bias0, L0_ffn_W1, L0_ffn_bias1):
    u_t, shared = host_prep(
        np.asarray(L0_attn_u, dtype=np.float32),
        np.asarray(L0_attn_Wqkv, dtype=np.float32),
        np.asarray(L0_attn_bqkv, dtype=np.float32),
        np.asarray(L0_attn_Wo, dtype=np.float32),
        np.asarray(L0_attn_bo, dtype=np.float32),
        np.asarray(L0_attn_sinks, dtype=np.float32),
        np.asarray(L0_ffn_W0, dtype=np.float32),
        np.asarray(L0_ffn_bias0, dtype=np.float32),
        np.asarray(L0_ffn_W1, dtype=np.float32),
        np.asarray(L0_ffn_bias1, dtype=np.float32))
    nc = build_nc()
    in_maps = []
    for c in range(NCORES):
        m = dict(shared)
        m["u_t"] = np.ascontiguousarray(u_t[c * NSEQ:(c + 1) * NSEQ])
        in_maps.append(m)
    res = run_bass_kernel_spmd(nc, in_maps, core_ids=list(range(NCORES)))
    outs = [r["out_t"] for r in res.results]
    out = np.concatenate(outs, axis=0).transpose(0, 2, 1)
    return np.ascontiguousarray(out).astype(np.float32)


if __name__ == "__main__":
    nc = build_nc()
    n = sum(1 for _ in nc.all_instructions())
    print("built ok,", n, "instructions")


# version 05
# speedup vs baseline: 1.4903x; 1.4903x over previous
import numpy as np
import ml_dtypes
from contextlib import ExitStack

import concourse.bass as bass
import concourse.mybir as mybir
from concourse.bass_utils import run_bass_kernel_spmd

F32 = mybir.dt.float32
BF16 = mybir.dt.bfloat16
AF = mybir.ActivationFunctionType
ALU = mybir.AluOpType
BFNP = ml_dtypes.bfloat16

NCORES = 8
NSEQ = 2
L = 2048
DM = 2048
NH = 32
NHG = 8
HPG = 4
TC = 512
NTC = L // TC
NKC = DM // 128
CEXP = 12.0
EPS = 1e-5

ENGINES = ("sync", "tensor", "vector", "scalar", "gpsimd")
import os
BUILD_PHASES = os.environ.get("KBUILD", "ABCD")
KD = int(os.environ.get("KD", "5"))


class Ctr:
    __slots__ = ("h", "n", "engine")

    def __init__(self, h, engine=None):
        self.h = h
        self.n = 0
        self.engine = engine


class Ph:
    def __init__(self, nc):
        self.nc = nc
        self.streams = {e: [] for e in ENGINES}
        self.waited = {e: {} for e in ENGINES}

    def wait(self, e, deps):
        todo = []
        for c, v in deps:
            if v <= 0 or c.engine == e:
                continue
            if self.waited[e].get(id(c), 0) >= v:
                continue
            self.waited[e][id(c)] = v
            todo.append((c.h, v))
        if todo:
            def f(eng, todo=todo):
                for h, v in todo:
                    eng.wait_ge(h, v)
            self.streams[e].append(f)

    def op(self, e, fn, inc, k=1, deps=()):
        self.wait(e, deps)
        inc.n += k
        def g(eng, fn=fn, h=inc.h, k=k):
            fn(eng).then_inc(h, k)
        self.streams[e].append(g)
        return inc.n

    def raw(self, e, fn):
        self.streams[e].append(fn)

    def build(self):
        with self.nc.Block() as blk:
            for e in ENGINES:
                fns = self.streams[e]
                if fns:
                    getattr(blk, e)(lambda eng, fns=fns: [f(eng) for f in fns])


def _band_mask(t):
    q0 = t * 128
    m = np.zeros((384, 128), dtype=np.float32)
    m[0:16] = 1.0
    kpos = q0 - 64 + np.arange(320)
    qpos = q0 + np.arange(128)
    band = (np.abs(kpos[:, None] - qpos[None, :]) <= 64) \
        & (kpos[:, None] >= 0) & (kpos[:, None] < L)
    m[64:384] = band.astype(np.float32)
    return m


def host_prep(u, Wqkv, bqkv, Wo, bo, sinks, W0, b0, W1, b1):
    d = {}
    u_t = np.ascontiguousarray(u.transpose(0, 2, 1))
    cols = []
    for hg in range(NHG):
        hs = [HPG * hg + j for j in range(HPG)]
        qA = [64 * h + dd for h in hs for dd in range(32)]
        qB = [64 * h + 32 + dd for h in hs for dd in range(32)]
        kA = [2048 + c for c in qA]
        kB = [2048 + c for c in qB]
        vv = [4096 + 64 * h + dd for h in hs for dd in range(64)]
        cols += qA + qB + kA + kB + vv
    cols = np.array(cols)
    d["wqkv"] = np.ascontiguousarray(Wqkv.T[:, cols])
    d["bqkv_pm"] = np.ascontiguousarray(bqkv[cols].reshape(48, 128).T)
    vrow = np.concatenate([bqkv[cols[hg * 768 + 512: hg * 768 + 768]]
                           for hg in range(NHG)])
    d["bqkv_vrow"] = np.ascontiguousarray(vrow[None, :])
    sa = np.zeros((128, NHG * 64), dtype=np.float32)
    sbv = np.zeros((128, NHG * 64), dtype=np.float32)
    spv = np.zeros((64, NHG * 260), dtype=np.float32)
    for hg in range(NHG):
        for j in range(HPG):
            h = HPG * hg + j
            sa[32 * j:32 * j + 32, hg * 64: hg * 64 + 16] = sinks[h, :, 0:32].T
            sbv[32 * j:32 * j + 32, hg * 64: hg * 64 + 16] = sinks[h, :, 32:64].T
            spv[0:16, hg * 260 + 65 * j: hg * 260 + 65 * j + 64] = sinks[h]
            spv[:, hg * 260 + 65 * j + 64] = 1.0
    d["sinks_a"] = sa
    d["sinks_b"] = sbv
    d["sinks_pv"] = spv
    d["wo_blk"] = np.ascontiguousarray(
        Wo.T.reshape(16, 128, 16, 128).transpose(2, 1, 0, 3)
        .reshape(16, 128, 2048))
    d["w0_blk"] = np.ascontiguousarray(
        W0.T.reshape(16, 128, 64, 128).transpose(2, 1, 0, 3)
        .reshape(64, 128, 2048))
    d["w1_blk"] = np.ascontiguousarray(
        W1.T.reshape(64, 128, 2, 1024).transpose(2, 0, 1, 3))
    d["bo_pm"] = np.ascontiguousarray(bo.reshape(16, 128).T)
    d["b0_pm"] = np.ascontiguousarray(b0.reshape(64, 128).T)
    d["b1_pm"] = np.ascontiguousarray(b1.reshape(16, 128).T)
    half = 32
    inv = 1.0 / (10000.0 ** (np.arange(half, dtype=np.float64) / half))
    ang = np.arange(L, dtype=np.float64)[:, None] * inv[None, :]
    cosT = np.tile(np.cos(ang).T, (4, 1)).astype(np.float32)
    sinT = np.tile(np.sin(ang).T, (4, 1)).astype(np.float32)
    d["rope_cq"] = (0.125 * cosT).astype(BFNP)
    d["rope_sq"] = (0.125 * sinT).astype(BFNP)
    d["rope_ck"] = cosT.astype(BFNP)
    d["rope_sk"] = sinT.astype(BFNP)
    mk = np.zeros((3, 128, 384), dtype=np.float32)
    for vi, t in enumerate([0, 1, 15]):
        m = _band_mask(t)
        for c in range(3):
            mk[vi, :, c * 128:(c + 1) * 128] = m[c * 128:(c + 1) * 128, :]
    d["mask3"] = ((mk - 1.0) * 1e9).astype(BFNP)
    d["id128"] = np.eye(128, dtype=np.float32).astype(BFNP)
    return u_t, d


def build_nc():
    nc = bass.Bass("TRN2", target_bir_lowering=False,
                   detect_race_conditions=(
                       os.environ.get("KRACE", "1") == "1"))

    u_t = nc.dram_tensor("u_t", [NSEQ, DM, L], F32, kind="ExternalInput").ap()
    wqkv = nc.dram_tensor("wqkv", [DM, 6144], F32, kind="ExternalInput").ap()
    bqkv_pm = nc.dram_tensor("bqkv_pm", [128, 48], F32, kind="ExternalInput").ap()
    bqkv_vrow = nc.dram_tensor("bqkv_vrow", [1, 2048], F32, kind="ExternalInput").ap()
    sinks_a = nc.dram_tensor("sinks_a", [128, 512], F32, kind="ExternalInput").ap()
    sinks_b = nc.dram_tensor("sinks_b", [128, 512], F32, kind="ExternalInput").ap()
    sinks_pv = nc.dram_tensor("sinks_pv", [64, 2080], F32, kind="ExternalInput").ap()
    wo_blk = nc.dram_tensor("wo_blk", [16, 128, 2048], F32, kind="ExternalInput").ap()
    w0_blk = nc.dram_tensor("w0_blk", [64, 128, 2048], F32, kind="ExternalInput").ap()
    w1_blk = nc.dram_tensor("w1_blk", [2, 64, 128, 1024], F32, kind="ExternalInput").ap()
    bo_pm = nc.dram_tensor("bo_pm", [128, 16], F32, kind="ExternalInput").ap()
    b0_pm = nc.dram_tensor("b0_pm", [128, 64], F32, kind="ExternalInput").ap()
    b1_pm = nc.dram_tensor("b1_pm", [128, 16], F32, kind="ExternalInput").ap()
    rope_cq = nc.dram_tensor("rope_cq", [128, L], BF16, kind="ExternalInput").ap()
    rope_sq = nc.dram_tensor("rope_sq", [128, L], BF16, kind="ExternalInput").ap()
    rope_ck = nc.dram_tensor("rope_ck", [128, L], BF16, kind="ExternalInput").ap()
    rope_sk = nc.dram_tensor("rope_sk", [128, L], BF16, kind="ExternalInput").ap()
    mask3d = nc.dram_tensor("mask3", [3, 128, 384], BF16, kind="ExternalInput").ap()
    id128d = nc.dram_tensor("id128", [128, 128], BF16, kind="ExternalInput").ap()
    out_t = nc.dram_tensor("out_t", [NSEQ, DM, L], F32, kind="ExternalOutput").ap()
    KDBG = os.environ.get("KDBG", "0") == "1"
    if KDBG:
        dbg_o = nc.dram_tensor("dbg_o", [NSEQ, NH, 64, L], BF16,
                               kind="ExternalOutput").ap()
        dbg_x1 = nc.dram_tensor("dbg_x1", [NSEQ, NKC, 128, L], F32,
                                kind="ExternalOutput").ap()
        dbg_st = nc.dram_tensor("dbg_st", [NSEQ, 4, NTC, 3, TC], F32,
                                kind="ExternalOutput").ap()
        dbg_ns = nc.dram_tensor("dbg_ns", [2, 4096], BF16,
                                kind="ExternalOutput").ap()

    wqkv_scr_h = nc.dram_tensor("wqkv_scr", [8, 128, 12288], BF16, kind="Internal")
    wo_scr_h = nc.dram_tensor("wo_scr", [16, 128, 2048], BF16, kind="Internal")
    w0_scr_h = nc.dram_tensor("w0_scr", [64, 128, 2048], BF16, kind="Internal")
    w1_scr_h = nc.dram_tensor("w1_scr", [2, 64, 128, 1024], BF16, kind="Internal")
    o_scr_h = nc.dram_tensor("o_scr", [NSEQ, NH, 64, L], BF16, kind="Internal")
    x1_scr_h = nc.dram_tensor("x1_scr", [NSEQ, NKC, 128, L], F32, kind="Internal")
    statrow_h = nc.dram_tensor("statrow", [NSEQ, 4, NTC, 3, TC], F32, kind="Internal")
    nsum_h = nc.dram_tensor("nsum_scr", [2, 4096], BF16, kind="Internal")
    rrow_h = nc.dram_tensor("rrow_scr", [NSEQ, NH, 2048], BF16, kind="Internal")

    S = ExitStack()
    with S:
        cT = Ctr(S.enter_context(nc.semaphore("cT")), "tensor")
        cA = Ctr(S.enter_context(nc.semaphore("cA")), "scalar")
        cD = Ctr(S.enter_context(nc.semaphore("cD")), "vector")
        cP = Ctr(S.enter_context(nc.semaphore("cP")), "gpsimd")
        cS = Ctr(S.enter_context(nc.semaphore("cS")), "sync")
        ECTR = {"sync": cS, "tensor": cT, "vector": cD, "scalar": cA,
                "gpsimd": cP}
        DSEMS = {}
        def dsem(name, q):
            c = Ctr(S.enter_context(nc.semaphore(name)))
            c.engine = None
            DSEMS[name] = (c, q)
            return c
        dU = dsem("dU", "sync")
        dSlab = dsem("dSlab", "sync")
        dTbl = dsem("dTbl", "sync")
        dOtc = dsem("dOtc", "sync")
        dWos = [dsem(f"dWos{j}", "sync") for j in range(4)]
        dW0s = [dsem(f"dW0s{j}", "sync") for j in range(4)]
        dW1s = [dsem(f"dW1s{j}", "sync") for j in range(4)]
        dWOS = [(c, "sync") for c in dWos]
        dW0S = [(c, "sync") for c in dW0s]
        dW1S = [(c, "sync") for c in dW1s]
        dX1w = dsem("dX1w", "gpsimd")
        dOw = dsem("dOw", "sync")
        dStat = dsem("dStat", "scalar")
        dNorm = dsem("dNorm", "scalar")
        dPre = dsem("dPre", "scalar")
        dPreX = dsem("dPreX", "scalar")
        dOut = dsem("dOut", "scalar")
        dP0L = dsem("dP0L", "scalar")
        dAcc = dsem("dAcc", "gpsimd")
        dAccX = dsem("dAccX", "gpsimd")
        dP0S = dsem("dP0S", "gpsimd")

        def sbt(name, shape, dt):
            return S.enter_context(nc.sbuf_tensor(name, shape, dt))

        pA = S.enter_context(nc.psum_tensor("pA", [128, 2048], F32))
        pB = S.enter_context(nc.psum_tensor("pB", [128, 2048], F32))

        A1 = sbt("A1", [128, 32768], BF16)
        A2 = sbt("A2", [128, 32768], BF16)
        A3b = sbt("A3", [128, 16384], BF16)
        A2f = A2.bitcast(F32)
        A3f = A3b.bitcast(F32)
        s1evac = A2f[0:1, 15360:16384]
        A1f = A1.bitcast(F32)
        s1evacC = A1f[0:1, 8192:9216]
        s1evacD = A2f[0:1, 14336:15360]

        bqkv_sb = sbt("bqkvt", [128, 48], F32)
        bo_sb = sbt("bot", [128, 16], F32)
        b0_sb = sbt("b0t", [128, 64], F32)
        b1_sb = sbt("b1t", [128, 16], F32)
        sA_sb = sbt("sAt", [128, 512], BF16)
        sB_sb = sbt("sBt", [128, 512], BF16)
        spv_sb = sbt("spvt", [64, 2080], BF16)
        bvrow_sb = sbt("bvrowt", [1, 2048], BF16)
        eps_t = sbt("epsT", [128, 1], F32)
        cexp_t = sbt("cexpT", [128, 1], F32)
        ones_c = sbt("onesC", [128, 1], BF16)
        ones_r = sbt("onesR", [1, 128], BF16)
        srows = sbt("srows", [1, 1536], F32)
        rc_in = sbt("rcin", [128, 64], BF16)
        rc_out = sbt("rcout", [128, 64], BF16)
        mnr_b = sbt("mnrb", [128, 2 * TC], F32)
        mneg_b = mnr_b[:, 0:TC]
        r_b = mnr_b[:, TC:2 * TC]
        id_sb = sbt("id128t", [128, 128], BF16)
        xsq_sb = sbt("xsq", [128, 1024], BF16)
        p0f = sbt("p0f", [128, 1536], F32)
        p0b = sbt("p0b", [128, 1536], BF16)

        h_sb = A1
        d_sb = A1
        u_sb = A3f
        xbfA = lambda: A2
        SLAB0 = 0
        QA0, QB0, KA0, KB0 = 12288, 14336, 16384, 18432
        ABF0, BBF0, VEXT0, PT0 = 20480, 22528, 24576, 28736
        OPAIR0, RNB0, MSK0 = 0, 4096, 6144
        CQ0, SQ0, CK0, SK0 = 7296, 9344, 11392, 13440
        OTC0, WOR0, XBFC0 = 0, 8192, 16384
        E0, H20, W0R0, W1R0, Y30, X20 = 0, 8192, 16384, 24576, 26624, 28672

        def bar_end(ph, engines):
            toks = {}
            for e in engines:
                c = ECTR[e]
                for nm, (dc, dq) in DSEMS.items():
                    if dq == e:
                        ph.wait(e, [(dc, dc.n)])
                def f(eng, c=c):
                    eng.drain()
                    eng.sem_inc(c.h, 1)
                c.n += 1
                ph.raw(e, f)
                toks[e] = c.n
            return toks

        def bar_start(ph, toks, engines):
            for e in engines:
                ph.wait(e, [(ECTR[e2], toks[e2]) for e2 in toks if e2 != e])

        def dma(ph, sem, out, in_, deps=(), accum=False, serial=False):
            ctr, q = DSEMS[sem] if isinstance(sem, str) else sem
            kw = {"accum_op": ALU.add} if accum else {}
            if serial:
                deps = list(deps) + [(ctr, ctr.n)]
            def f(eng, out=out, in_=in_, kw=kw):
                return eng.dma_start(out=out, in_=in_, **kw)
            return ph.op(q, f, inc=ctr, k=16, deps=deps)

        state = {"statrow_dma": 0, "stats_ev_cd": 0}

        KST = int(os.environ.get("KST", "9"))
        def ln_stats(ph, s, site, tc, xap, chunk_deps, want_mnegr, gate=False,
                     sev=None):
            sev = s1evac if sev is None else sev
            s2_toks = []
            for kc in range(NKC):
                dep = list(chunk_deps(kc))
                if kc == 0:
                    dep.append((cD, state["stats_ev_cd"]))
                if kc >= 2:
                    dep.append((cT, s2_toks[kc - 2]))
                tk = ph.op("vector", lambda eng, kc=kc: eng.tensor_mul(
                    out=xsq_sb[:, (kc % 2) * 512:(kc % 2 + 1) * 512],
                    in0=xap(kc), in1=xap(kc)), inc=cD, deps=dep)
                ph.op("tensor", lambda eng, kc=kc: eng.matmul(
                    pA[0:1, 1024:1536], ones_c[:], xap(kc),
                    start=(kc == 0), stop=(kc == NKC - 1)),
                    inc=cT, deps=dep)
                t2 = ph.op("tensor", lambda eng, kc=kc: eng.matmul(
                    pA[0:1, 1536:2048], ones_c[:],
                    xsq_sb[:, (kc % 2) * 512:(kc % 2 + 1) * 512],
                    start=(kc == 0), stop=(kc == NKC - 1)),
                    inc=cT, deps=[(cD, tk)])
                s2_toks.append(t2)
            if gate and KST < 2:
                return state["statrow_dma"]
            tev = ph.op("vector", lambda eng: eng.tensor_copy(
                out=sev, in_=pA[0:1, 1024:2048]), inc=cD,
                deps=[(cT, s2_toks[-1]), (dStat, state["statrow_dma"])])
            state["stats_ev_cd"] = tev
            t_mneg = ph.op("vector", lambda eng: eng.tensor_scalar_mul(
                out=srows[0:1, 0:512], in0=sev[0:1, 0:512],
                scalar1=-1.0 / DM), inc=cD)
            ph.op("vector", lambda eng: eng.tensor_mul(
                out=srows[0:1, 1024:1536], in0=srows[0:1, 0:512],
                in1=srows[0:1, 0:512]), inc=cD)
            t_v = ph.op("vector", lambda eng: eng.scalar_tensor_tensor(
                out=sev[0:1, 512:1024], in0=sev[0:1, 512:1024],
                scalar=1.0 / DM, in1=srows[0:1, 1024:1536],
                op0=ALU.mult, op1=ALU.subtract), inc=cD)
            if gate and KST < 3:
                return state["statrow_dma"]
            ph.op("scalar", lambda eng: eng.activation(
                out=srows[0:1, 1024:1536], in_=sev[0:1, 512:1024],
                func=AF.Ln, bias=eps_t[0:1, :]), inc=cA, deps=[(cD, t_v)])
            t_r = ph.op("scalar", lambda eng: eng.activation(
                out=srows[0:1, 512:1024], in_=srows[0:1, 1024:1536],
                func=AF.Exp, scale=-0.5), inc=cA)
            deps_w = [(cA, t_r), (cD, t_mneg)]
            if want_mnegr:
                t_mr = ph.op("vector", lambda eng: eng.tensor_mul(
                    out=srows[0:1, 1024:1536], in0=srows[0:1, 0:512],
                    in1=srows[0:1, 512:1024]), inc=cD, deps=[(cA, t_r)])
                deps_w = [(cD, t_mr)]
            if gate and KST < 4:
                return state["statrow_dma"]
            tw = dma(ph, "dStat",
                     statrow_h[s, site, tc].rearrange("a c -> (a c)").unsqueeze(0),
                     srows[0:1, 0:1536], deps=deps_w, serial=True)
            state["statrow_dma"] = tw
            return tw

        def bcast(ph, dst, s, site, tc, kind, deps, n=1):
            src = statrow_h[s, site, tc].rearrange("a c -> (a c)")[
                kind * TC:(kind + n) * TC].unsqueeze(0).broadcast_to(
                [128, n * TC])
            return dma(ph, "dStat", dst, src, deps=deps, serial=True)

        NB = [e for e in ENGINES if e != "gpsimd"]

        ph = Ph(nc)
        for dst, src in [(bqkv_sb[:], bqkv_pm), (bo_sb[:], bo_pm),
                         (b0_sb[:], b0_pm), (b1_sb[:], b1_pm)]:
            dma(ph, "dStat", dst, src, serial=True)
        t1 = dma(ph, "dStat", A2f[:, 0:512], sinks_a, serial=True)
        ph.op("vector", lambda eng: eng.tensor_copy(
            out=sA_sb[:], in_=A2f[:, 0:512]), inc=cD, deps=[(dStat, t1)])
        t2 = dma(ph, "dStat", A2f[:, 512:1024], sinks_b, serial=True)
        ph.op("vector", lambda eng: eng.tensor_copy(
            out=sB_sb[:], in_=A2f[:, 512:1024]), inc=cD, deps=[(dStat, t2)])
        t3 = dma(ph, "dStat", A2f[0:64, 1024:3104], sinks_pv, serial=True)
        ph.op("vector", lambda eng: eng.tensor_copy(
            out=spv_sb[:], in_=A2f[0:64, 1024:3104]), inc=cD, deps=[(dStat, t3)])
        t5 = dma(ph, "dStat", A2f[0:1, 3104:5152], bqkv_vrow, serial=True)
        ph.op("vector", lambda eng: eng.tensor_copy(
            out=bvrow_sb[:], in_=A2f[0:1, 3104:5152]), inc=cD, deps=[(dStat, t5)])
        dma(ph, "dStat", id_sb[:], id128d, serial=True)
        ph.op("vector", lambda eng: eng.memset(eps_t[:], EPS), inc=cD)
        ph.op("vector", lambda eng: eng.memset(cexp_t[:], -CEXP), inc=cD)
        ph.op("vector", lambda eng: eng.memset(ones_c[:], 1.0), inc=cD)
        ph.op("vector", lambda eng: eng.memset(ones_r[:], 1.0), inc=cD)
        prev_toks = bar_end(ph, ENGINES)
        ph.build()

        p0_items = []
        for hg in range(NHG):
            for kc in range(NKC):
                p0_items.append((wqkv[kc * 128:(kc + 1) * 128,
                                      hg * 768:(hg + 1) * 768],
                                 wqkv_scr_h[hg][:, kc * 768:(kc + 1) * 768]))
        for mc in range(16):
            for pc in range(4):
                p0_items.append((wo_blk[mc][:, pc * 512:(pc + 1) * 512],
                                 wo_scr_h[mc][:, pc * 512:(pc + 1) * 512]))
        for mc in range(64):
            for pc in range(4):
                p0_items.append((w0_blk[mc][:, pc * 512:(pc + 1) * 512],
                                 w0_scr_h[mc][:, pc * 512:(pc + 1) * 512]))
        for oh in range(2):
            for kc in range(64):
                for pc in range(2):
                    p0_items.append(
                        (w1_blk[oh, kc][:, pc * 512:(pc + 1) * 512],
                         w1_scr_h[oh, kc][:, pc * 512:(pc + 1) * 512]))
        p0 = {"li": 0, "ci": 0, "store": [0] * len(p0_items),
              "conv": [0] * len(p0_items),
              "ltok": [0] * len(p0_items)}

        def p0_dep_blk(i0):
            return p0["store"][i0]

        def p0_pump(ph, n):
            N = len(p0_items)
            def load_one(i):
                src, dst = p0_items[i]
                o0 = (i % 2) * 768
                w = src.shape[1]
                depl = [(cP, p0["conv"][i - 2])] if i >= 2 else []
                p0["ltok"][i] = dma(ph, "dP0L", p0f[:, o0:o0 + w], src,
                                    deps=depl)
            for _ in range(n):
                while p0["li"] < N and p0["li"] < p0["ci"] + 2:
                    load_one(p0["li"])
                    p0["li"] += 1
                i = p0["ci"]
                if i >= N:
                    return
                p0["ci"] += 1
                src, dst = p0_items[i]
                o0 = (i % 2) * 768
                w = src.shape[1]
                deps_c = [(dP0L, p0["ltok"][i] + (16 if i + 1 < N else 0))]
                if i >= 1:
                    deps_c.append((dP0S, p0["store"][i - 1]))
                tcv = ph.op("gpsimd", lambda eng, o0=o0, w=w:
                            eng.tensor_copy(out=p0b[:, o0:o0 + w],
                                            in_=p0f[:, o0:o0 + w]),
                            inc=cP, deps=deps_c)
                ts = dma(ph, "dP0S", dst, p0b[:, o0:o0 + w])
                p0["conv"][i] = tcv
                p0["store"][i] = ts

        x1stat = {}

        nseq_build = NSEEQ if False else (NSEQ if "D" in BUILD_PHASES or True else 1)
        nseq_build = int(os.environ.get("KSEQ", str(NSEQ)))
        for s in range(nseq_build):
            if "A" not in BUILD_PHASES:
                continue
            ph = Ph(nc)
            bar_start(ph, prev_toks, NB)
            prev_cd = 0
            prev_ct = 0
            for tc in range(NTC):
                ld = []
                for kc in range(NKC):
                    dep = [(cD, prev_cd)] if kc == 0 else []
                    ld.append(dma(ph, "dU", u_sb[:, kc * TC:(kc + 1) * TC],
                                  u_t[s, kc * 128:(kc + 1) * 128,
                                      tc * TC:(tc + 1) * TC], deps=dep))
                cps = []
                for kc in range(NKC):
                    dep = [(dU, ld[-1])]
                    if kc == 0:
                        dep.append((cT, prev_ct))
                    cps.append(ph.op("scalar", lambda eng, kc=kc:
                                     eng.activation(
                            out=A2[:, kc * TC:(kc + 1) * TC],
                            in_=u_sb[:, kc * TC:(kc + 1) * TC], func=AF.Copy),
                            inc=cA, deps=dep))
                tw = ln_stats(ph, s, 0, tc,
                              lambda kc: A2[:, kc * TC:(kc + 1) * TC],
                              lambda kc: [(cA, cps[kc])], False)
                prev_ct = cT.n
                tb1 = bcast(ph, mnr_b[:], s, 0, tc, 0,
                            deps=[(dStat, tw), (cD, cD.n)], n=2)
                tb2 = tb1
                for kc in range(NKC):
                    ph.op("vector", lambda eng, kc=kc: eng.tensor_add(
                        out=u_sb[:, kc * TC:(kc + 1) * TC],
                        in0=u_sb[:, kc * TC:(kc + 1) * TC], in1=mneg_b[:]),
                        inc=cD, deps=[(dStat, tb1)])
                    ph.op("vector", lambda eng, kc=kc, tc=tc: eng.tensor_mul(
                        out=h_sb[:, kc * L + tc * TC: kc * L + (tc + 1) * TC],
                        in0=u_sb[:, kc * TC:(kc + 1) * TC], in1=r_b[:]),
                        inc=cD, deps=[(dStat, tb2)])
                prev_cd = cD.n
                if s == 0:
                    p0_pump(ph, 40)
            prev_toks = bar_end(ph, NB)
            ph.build()

            if "B" not in BUILD_PHASES:
                continue
            ph = Ph(nc)
            bar_start(ph, prev_toks, NB)
            tbl_toks = []
            for off, src in [(CQ0, rope_cq), (SQ0, rope_sq),
                             (CK0, rope_ck), (SK0, rope_sk)]:
                tbl_toks.append(dma(ph, "dTbl", A3b[:, off:off + 2048], src))
            tmsk = dma(ph, "dTbl",
                       A3b[:, MSK0:MSK0 + 1152].rearrange(
                           "p (v c) -> p v c", v=3),
                       mask3d.transpose([1, 0, 2]))
            tbl_all = tmsk
            slab_cons = 0
            exp_tok = [0, 0]
            pv_tok = [0, 0]
            oev_tok = [0, 0]
            att_done_ct = 0
            last_oscr = 0
            opair = A3b[0:65, OPAIR0:OPAIR0 + 4096]

            def tabap(off):
                return A3b[:, off:off + 2048]

            for hg in range(NHG):
                depl = [(cT, slab_cons)]
                if s == 0:
                    depl.append((dP0S, p0["store"][hg * 16 + 15] + 16))
                tslab = dma(ph, "dSlab", A2[:, SLAB0:SLAB0 + 12288],
                            wqkv_scr_h[hg], deps=depl)
                rope_final = {}
                rope_abf = {}
                evk = {}
                for mi, (dA0, dB0, tC0, tS0) in enumerate(
                        [(QA0, QB0, CQ0, SQ0), (KA0, KB0, CK0, SK0)]):
                    dsta = A2[:, dA0:dA0 + 2048]
                    dstb = A2[:, dB0:dB0 + 2048]
                    tabC = tabap(tC0)
                    tabS = tabap(tS0)
                    abf = A2[:, ABF0:ABF0 + 2048]
                    bbf = A2[:, BBF0:BBF0 + 2048]
                    mm_last = {}
                    for half in range(2):
                        mci = mi * 2 + half
                        ps = pA if half == 0 else pB
                        for tcq in range(NTC):
                            for kc in range(NKC):
                                dep = [(dSlab, tslab)]
                                if tcq == 0 and kc == 0:
                                    if mi == 0:
                                        dep += [(cA, exp_tok[0]),
                                                (cA, exp_tok[1]),
                                                (cD, oev_tok[0]),
                                                (cD, oev_tok[1]),
                                                (cT, pv_tok[0]),
                                                (cT, pv_tok[1])]
                                    else:
                                        dep += [(cA, evk[(0, half)]),
                                                (cD, rope_abf[0])]
                                ph.op("tensor", lambda eng, ps=ps, tcq=tcq,
                                      kc=kc, mci=mci: eng.matmul(
                                        ps[:, tcq * 512:(tcq + 1) * 512],
                                        A2[:, SLAB0 + kc * 768 + mci * 128:
                                           SLAB0 + kc * 768 + (mci + 1) * 128],
                                        h_sb[:, kc * L + tcq * TC:
                                             kc * L + (tcq + 1) * TC],
                                        start=(kc == 0), stop=(kc == NKC - 1)),
                                      inc=cT, deps=dep)
                        mm_last[half] = cT.n
                    bca = bqkv_sb[:, hg * 6 + mi * 2: hg * 6 + mi * 2 + 1]
                    bcb = bqkv_sb[:, hg * 6 + mi * 2 + 1: hg * 6 + mi * 2 + 2]
                    depe = [(cT, mm_last[0])]
                    depe2 = [(cT, mm_last[1])]
                    if mi == 1:
                        depe.append((cD, rope_abf[0]))
                        depe2.append((cD, rope_abf[0]))
                    te1 = ph.op("scalar", lambda eng, bca=bca, abf=abf:
                                eng.activation(out=abf, in_=pA[:],
                                               func=AF.Identity, bias=bca),
                                inc=cA, deps=depe)
                    te2 = ph.op("scalar", lambda eng, bcb=bcb, bbf=bbf:
                                eng.activation(out=bbf, in_=pB[:],
                                               func=AF.Identity, bias=bcb),
                                inc=cA, deps=depe2)
                    evk[(mi, 0)] = te1
                    evk[(mi, 1)] = te2
                    d1 = [(cA, te1), (cA, te2), (dTbl, tbl_all)]
                    ph.op("vector", lambda eng, dsta=dsta, abf=abf, tabC=tabC:
                          eng.tensor_mul(out=dsta, in0=abf, in1=tabC),
                          inc=cD, deps=d1)
                    ph.op("vector", lambda eng, dstb=dstb, bbf=bbf, tabS=tabS:
                          eng.tensor_mul(out=dstb, in0=bbf, in1=tabS), inc=cD)
                    ph.op("vector", lambda eng, dsta=dsta, dstb=dstb:
                          eng.tensor_sub(out=dsta, in0=dsta, in1=dstb), inc=cD)
                    ta4 = ph.op("vector", lambda eng, abf=abf, tabS=tabS:
                                eng.tensor_mul(out=abf, in0=abf, in1=tabS),
                                inc=cD)
                    ph.op("vector", lambda eng, dstb=dstb, bbf=bbf, tabC=tabC:
                          eng.tensor_mul(out=dstb, in0=bbf, in1=tabC), inc=cD)
                    trp = ph.op("vector", lambda eng, dstb=dstb, abf=abf:
                                eng.tensor_add(out=dstb, in0=abf, in1=dstb),
                                inc=cD)
                    rope_abf[mi] = trp
                    rope_final[mi] = trp
                vext0 = VEXT0
                tms = ph.op("vector", lambda eng: eng.memset(
                    A2[:, vext0:vext0 + 4160], 1.0), inc=cD,
                    deps=[(cT, att_done_ct)])
                vevac = []
                for tb in range(16):
                    slot = tb % 8
                    ps = pA if slot < 4 else pB
                    col = (slot % 4) * 512
                    dep0 = [(cA, evk[(1, 0)]), (cA, evk[(1, 1)]),
                            (cD, rope_abf[1])]
                    if tb >= 8:
                        dep0.append((cA, vevac[tb - 8]))
                    ph.op("tensor", lambda eng, ps=ps, col=col, hg=hg:
                          eng.matmul(ps[:, col:col + 256], ones_r[:],
                                     bvrow_sb[0:1, hg * 256:(hg + 1) * 256],
                                     start=True, stop=False),
                          inc=cT, deps=dep0)
                    tbm = 0
                    for kc in range(NKC):
                        tbm = ph.op("tensor", lambda eng, ps=ps, col=col,
                                    kc=kc, tb=tb: eng.matmul(
                                ps[:, col:col + 256],
                                h_sb[:, kc * L + tb * 128:
                                     kc * L + tb * 128 + 128],
                                A2[:, SLAB0 + kc * 768 + 512:
                                   SLAB0 + kc * 768 + 768],
                                start=False, stop=(kc == NKC - 1)),
                            inc=cT, deps=[(dSlab, tslab)])
                    tve = ph.op("scalar", lambda eng, ps=ps, col=col, tb=tb:
                                eng.activation(
                            out=A2[:, vext0 + tb * 260:
                                   vext0 + tb * 260 + 260].rearrange(
                                "p (j dd) -> p j dd", j=4)[:, :, 0:64],
                            in_=ps[:, col:col + 256].rearrange(
                                "p (j dd) -> p j dd", j=4),
                            func=AF.Copy), inc=cA, deps=[(cT, tbm), (cD, tms)])
                    vevac.append(tve)
                slab_cons = cT.n
                for pp in range(2):
                    for t in range(16):
                        slot = t % 2
                        sbase = slot * 1024
                        q0 = t * 128
                        mt = 0 if t == 0 else (2 if t == 15 else 1)
                        dep_first = [(cD, rope_final[0]), (cD, rope_final[1]),
                                     (cA, exp_tok[slot]), (cA, vevac[15])]
                        first = True
                        for jj in range(2):
                            j = pp * 2 + jj
                            b = 32 * j
                            base = sbase + jj * 512
                            w0s = q0 - 64 if t > 0 else 0
                            w2s = q0 + 128 if t < 15 else q0
                            ka = A2[:, KA0:KA0 + 2048]
                            kb = A2[:, KB0:KB0 + 2048]
                            groups = [
                                (sA_sb[b:b + 32, hg * 64: hg * 64 + 64],
                                 sB_sb[b:b + 32, hg * 64: hg * 64 + 64],
                                 pA[0:64, base:base + 128], (b, 0)),
                                (ka[b:b + 32, w0s:w0s + 64],
                                 kb[b:b + 32, w0s:w0s + 64],
                                 pA[64:128, base:base + 128], (b, 64)),
                                (ka[b:b + 32, q0:q0 + 128],
                                 kb[b:b + 32, q0:q0 + 128],
                                 pA[0:128, base + 128:base + 256], (b, 0)),
                                (ka[b:b + 32, w2s:w2s + 128],
                                 kb[b:b + 32, w2s:w2s + 128],
                                 pA[0:128, base + 256:base + 384], (b, 0)),
                            ]
                            qaap = A2[b:b + 32, QA0 + q0:QA0 + q0 + 128]
                            qbap = A2[b:b + 32, QB0 + q0:QB0 + q0 + 128]
                            mregs = ((0, 64, 0), (64, 128, 0),
                                     (0, 128, 128), (0, 128, 256))
                            for gi, (lA, lB, outap, tp) in enumerate(groups):
                                mp0, mp1, rc = mregs[gi]
                                mc = MSK0 + mt * 384 + rc
                                ph.op("tensor", lambda eng, base=base, mc=mc,
                                      mp0=mp0, mp1=mp1, rc=rc: eng.matmul(
                                        pA[mp0:mp1, base + rc:base + rc + 128],
                                        id_sb[mp0:mp1, mp0:mp1],
                                        A3b[mp0:mp1, mc:mc + 128],
                                        start=True, stop=False,
                                        tile_position=(mp0, mp0)),
                                      inc=cT,
                                      deps=(dep_first + [(dTbl, tbl_all)])
                                      if first else [])
                                first = False
                                ph.op("tensor", lambda eng, outap=outap, lA=lA,
                                      qaap=qaap, tp=tp: eng.matmul(
                                        outap, lA, qaap, start=False,
                                        stop=False, tile_position=tp),
                                      inc=cT)
                                ph.op("tensor", lambda eng, outap=outap, lB=lB,
                                      qbap=qbap, tp=tp: eng.matmul(
                                        outap, lB, qbap, start=False,
                                        stop=True, tile_position=tp), inc=cT)
                        tmm_last = cT.n
                        texp = ph.op("scalar", lambda eng, sbase=sbase,
                                     slot=slot: eng.activation(
                                out=A2[:, PT0 + slot * 768:
                                       PT0 + (slot + 1) * 768].rearrange(
                                    "p (hh c) -> p hh c", hh=2),
                                in_=pA[:, sbase:sbase + 1024].rearrange(
                                    "p (hh c) -> p hh c", hh=2)[:, :, 0:384],
                                func=AF.Exp, bias=cexp_t[:]), inc=cA,
                                deps=[(cT, tmm_last), (cT, pv_tok[slot])])
                        exp_tok[slot] = texp
                        obank = (t // 2) % 2
                        ocol = obank * 512 + (t % 2) * 256
                        blkm = max(t - 1, 0)
                        blkp = min(t + 1, 15)
                        for jj in range(2):
                            j = pp * 2 + jj
                            oreg = pB[0:65, ocol + jj * 128:
                                      ocol + jj * 128 + 128]
                            ptb = PT0 + slot * 768 + 384 * jj
                            dep_pv = [(cA, texp), (cD, oev_tok[obank]),
                                      (cA, vevac[blkp]), (cA, vevac[blkm]),
                                      (cA, vevac[t])]
                            ph.op("tensor", lambda eng, oreg=oreg, hg=hg, j=j,
                                  ptb=ptb: eng.matmul(
                                    oreg,
                                    spv_sb[:, hg * 260 + 65 * j:
                                           hg * 260 + 65 * j + 65],
                                    A2[0:64, ptb:ptb + 128],
                                    start=True, stop=False,
                                    tile_position=(0, 0)), inc=cT,
                                  deps=dep_pv)
                            ph.op("tensor", lambda eng, oreg=oreg, t=t, j=j,
                                  ptb=ptb, vext0=vext0: eng.matmul(
                                    oreg,
                                    A2[:, vext0 + t * 260 + 65 * j:
                                       vext0 + t * 260 + 65 * j + 65],
                                    A2[:, ptb + 128:ptb + 256],
                                    start=False, stop=False), inc=cT)
                            ph.op("tensor", lambda eng, oreg=oreg, blkm=blkm,
                                  j=j, ptb=ptb, vext0=vext0: eng.matmul(
                                    oreg,
                                    A2[64:128, vext0 + blkm * 260 + 65 * j:
                                       vext0 + blkm * 260 + 65 * j + 65],
                                    A2[64:128, ptb:ptb + 128],
                                    start=False, stop=False,
                                    tile_position=(64, 0)), inc=cT)
                            pv_tok[slot] = ph.op(
                                "tensor", lambda eng, oreg=oreg, blkp=blkp,
                                j=j, ptb=ptb, vext0=vext0: eng.matmul(
                                    oreg,
                                    A2[:, vext0 + blkp * 260 + 65 * j:
                                       vext0 + blkp * 260 + 65 * j + 65],
                                    A2[:, ptb + 256:ptb + 384],
                                    start=False, stop=True), inc=cT)
                        if t % 2 == 1:
                            t0 = t - 1
                            depv = [(cT, pv_tok[slot]), (dOw, last_oscr)]
                            tev = ph.op(
                                "vector", lambda eng, obank=obank, t0=t0:
                                eng.tensor_copy(
                                    out=opair.rearrange(
                                        "p (j c) -> p j c", j=2)[
                                        :, :, t0 * 128:t0 * 128 + 256]
                                    .rearrange(
                                        "p j (tt c) -> p j tt c", tt=2),
                                    in_=pB[0:65, obank * 512:
                                           (obank + 1) * 512].rearrange(
                                        "p (tt j c) -> p j tt c",
                                        tt=2, j=2)), inc=cD, deps=depv)
                            oev_tok[obank] = tev
                    for jj in range(2):
                        j = pp * 2 + jj
                        last_oscr = dma(ph, "dOw",
                                        o_scr_h[s, hg * 4 + j],
                                        opair[0:64, jj * 2048:(jj + 1) * 2048],
                                        deps=[(cD, oev_tok[0]),
                                              (cD, oev_tok[1])])
                    tn1 = dma(ph, "dNorm", nsum_h[pp].unsqueeze(0),
                              opair[64:65, 0:4096],
                              deps=[(cD, oev_tok[0]), (cD, oev_tok[1])],
                              serial=True)
                    tn2 = dma(ph, "dNorm", rc_in[:, 0:32],
                              nsum_h[pp].rearrange("(p c) -> p c", p=128),
                              deps=[(dNorm, tn1), (cD, cD.n)], serial=True)
                    def _recip(eng):
                        with nc.allow_low_precision(
                                reason="softmax denominators kept bf16"):
                            return eng.reciprocal(out=rc_out[:, 0:32],
                                                  in_=rc_in[:, 0:32])
                    trc = ph.op("vector", _recip, inc=cD,
                                deps=[(dNorm, tn2)])
                    tn3 = dma(ph, "dNorm",
                              rrow_h[s, hg * 4 + pp * 2:hg * 4 + pp * 2 + 2]
                              .rearrange("h t -> (h t)")
                              .rearrange("(p c) -> p c", p=128),
                              rc_out[:, 0:32], deps=[(cD, trc)], serial=True)
                att_done_ct = cT.n
                if s == 0:
                    p0_pump(ph, 70)
            if s == 0:
                p0_pump(ph, len(p0_items))
            prev_toks = bar_end(ph, NB)
            ph.build()

            if "C" not in BUILD_PHASES:
                continue
            ph = Ph(nc)
            bar_start(ph, prev_toks, NB)
            ph.wait("gpsimd", [(ECTR[e], prev_toks[e]) for e in prev_toks])
            wo_mm_prev = 0
            prev_trb = 0
            wo_last_mm = [0] * 16
            for tc in range(NTC):
                o_toks = []
                for hh in range(NH):
                    dep = [(cT, wo_mm_prev)] if hh == 0 else []
                    o_toks.append(dma(
                        ph, "dOtc",
                        A2[(hh % 2) * 64:(hh % 2) * 64 + 64,
                           OTC0 + (hh // 2) * TC: OTC0 + (hh // 2 + 1) * TC],
                        o_scr_h[s, hh][:, tc * TC:(tc + 1) * TC], deps=dep))
                RB0c = 24576
                for hh in range(NH):
                    o_toks.append(dma(
                        ph, "dOtc",
                        A2[(hh % 2) * 64:(hh % 2) * 64 + 64,
                           RB0c + (hh // 2) * TC: RB0c + (hh // 2 + 1) * TC],
                        rrow_h[s, hh][tc * TC:(tc + 1) * TC].unsqueeze(0)
                        .broadcast_to([64, TC]),
                        deps=[(cD, prev_trb)] if hh == 0 else []))
                o_all = o_toks[-1]
                prev_trb = ph.op("vector", lambda eng: eng.tensor_mul(
                    out=A2[:, OTC0:OTC0 + 8192],
                    in0=A2[:, OTC0:OTC0 + 8192],
                    in1=A2[:, RB0c:RB0c + 8192]), inc=cD,
                    deps=[(dOtc, o_all)])
                ev_toks = []
                mm_toks = []
                wol = []
                for mc in range(16):
                    wbuf = mc % 4
                    dep_w = [(cT, mm_toks[mc - 4])] if mc >= 4 else \
                        [(cT, wo_last_mm[12 + mc])]
                    if s == 0 and tc == 0 and mc < 4:
                        dep_w.append((dP0S, 16 * len(p0_items)))
                    twl = dma(ph, dWOS[wbuf],
                              A2[:, WOR0 + wbuf * 2048:
                                 WOR0 + (wbuf + 1) * 2048],
                              wo_scr_h[mc], deps=dep_w)
                    wol.append(twl)
                    ps = pA if mc % 2 == 0 else pB
                    tmm = 0
                    for kc in range(NKC):
                        dep = [(dWos[wbuf], twl), (cD, prev_trb)]
                        if mc >= 2 and kc == 0:
                            dep.append((cA, ev_toks[mc - 2]))
                        tmm = ph.op("tensor", lambda eng, ps=ps, wbuf=wbuf,
                                    kc=kc: eng.matmul(
                                ps[:, 0:512],
                                A2[:, WOR0 + wbuf * 2048 + kc * 128:
                                   WOR0 + wbuf * 2048 + (kc + 1) * 128],
                                A2[:, OTC0 + kc * TC: OTC0 + (kc + 1) * TC],
                                start=(kc == 0), stop=(kc == NKC - 1)),
                                inc=cT, deps=dep)
                    mm_toks.append(tmm)
                    ev_toks.append(ph.op("scalar", lambda eng, ps=ps, mc=mc:
                                         eng.activation(
                            out=A2[:, XBFC0 + mc * TC: XBFC0 + (mc + 1) * TC],
                            in_=ps[:, 0:512], func=AF.Identity,
                            bias=bo_sb[:, mc:mc + 1]), inc=cA,
                            deps=[(cT, tmm)]))
                wo_mm_prev = cT.n
                wo_last_mm = mm_toks
                tw = ln_stats(ph, s, 1, tc,
                              lambda kc: A2[:, XBFC0 + kc * TC:
                                            XBFC0 + (kc + 1) * TC],
                              lambda kc: [(cA, ev_toks[kc])], False,
                              sev=s1evacC)
                tb1 = bcast(ph, mnr_b[:], s, 1, tc, 0,
                            deps=[(dStat, tw), (cD, cD.n)], n=2)
                tb2 = tb1
                x1a = []
                for kc in range(NKC):
                    dep1 = [(dStat, tb1)]
                    if kc == 0:
                        dep1 += [(dAcc, dAcc.n), (dX1w, dX1w.n)]
                    ph.op("vector", lambda eng, kc=kc: eng.tensor_add(
                        out=u_sb[:, kc * TC:(kc + 1) * TC],
                        in0=A2[:, XBFC0 + kc * TC: XBFC0 + (kc + 1) * TC],
                        in1=mneg_b[:]), inc=cD, deps=dep1)
                    t2 = ph.op("vector", lambda eng, kc=kc: eng.tensor_mul(
                        out=u_sb[:, kc * TC:(kc + 1) * TC],
                        in0=u_sb[:, kc * TC:(kc + 1) * TC], in1=r_b[:]),
                        inc=cD, deps=[(dStat, tb2)])
                    x1a.append(dma(ph, "dAcc",
                                   u_sb[:, kc * TC:(kc + 1) * TC],
                                   u_t[s, kc * 128:(kc + 1) * 128,
                                       tc * TC:(tc + 1) * TC],
                                   deps=[(cD, t2)], accum=True))
                cp2 = []
                for kc in range(NKC):
                    cp2.append(ph.op("scalar", lambda eng, kc=kc:
                                     eng.activation(
                            out=A1[:, kc * TC:(kc + 1) * TC],
                            in_=u_sb[:, kc * TC:(kc + 1) * TC], func=AF.Copy),
                            inc=cA, deps=[(dAcc, x1a[-1])]))
                    dma(ph, "dX1w",
                        x1_scr_h[s, kc][:, tc * TC:(tc + 1) * TC],
                        u_sb[:, kc * TC:(kc + 1) * TC],
                        deps=[(dAcc, x1a[-1])])
                x1stat[(s, tc)] = ln_stats(
                    ph, s, 2, tc,
                    lambda kc: A1[:, kc * TC:(kc + 1) * TC],
                    lambda kc: [(cA, cp2[kc])], False, sev=s1evacC)
            prev_toks = bar_end(ph, ENGINES)
            ph.build()

            if "D" not in BUILD_PHASES:
                continue
            ph = Ph(nc)
            bar_start(ph, prev_toks, ENGINES)
            h2_prev = 0
            e_prev = 0
            x2_last_cd = 0
            x2_last_dout = 0
            w0_last_mm = [0] * 64
            w1_hist4 = [0] * 4
            w0_final = 0
            for tc in range(NTC):
                tb2 = bcast(ph, r_b, s, 2, tc, 1,
                            deps=[(dStat, x1stat[(s, tc)]), (cD, cD.n)])
                pre_toks = []
                for kc in range(NKC):
                    dep = [(cD, h2_prev)] if kc == 0 else []
                    pre_toks.append(dma(
                        ph, "dPre", u_sb[:, kc * TC:(kc + 1) * TC],
                        statrow_h[s, 2, tc, 0].unsqueeze(0)
                        .broadcast_to([128, TC]), deps=dep))
                acc_toks = []
                for kc in range(NKC):
                    acc_toks.append(dma(
                        ph, "dAcc", u_sb[:, kc * TC:(kc + 1) * TC],
                        x1_scr_h[s, kc][:, tc * TC:(tc + 1) * TC],
                        deps=[(dPre, pre_toks[-1])], accum=True))
                h2_toks = []
                for kc in range(NKC):
                    h2_toks.append(ph.op("vector", lambda eng, kc=kc:
                                         eng.tensor_mul(
                            out=A2[:, H20 + kc * TC: H20 + (kc + 1) * TC],
                            in0=u_sb[:, kc * TC:(kc + 1) * TC], in1=r_b[:]),
                            inc=cD, deps=[(dAcc, acc_toks[-1]),
                                          (dStat, tb2), (cT, w0_final)]))
                h2_prev = h2_toks[-1]
                g_toks = []
                w0mm = []
                for mc in range(64):
                    wbuf = mc % 4
                    dep_w = [(cT, w0mm[mc - 4])] if mc >= 4 else \
                        [(cT, w0_last_mm[60 + mc])]
                    if s == 0 and tc == 0 and mc < 4:
                        dep_w.append((dP0S, 16 * len(p0_items)))
                    twl = dma(ph, dW0S[wbuf],
                              A2[:, W0R0 + wbuf * 2048:
                                 W0R0 + (wbuf + 1) * 2048],
                              w0_scr_h[mc], deps=dep_w)
                    ps = pA if mc % 2 == 0 else pB
                    pcol = ((mc % 4) // 2) * 512
                    tmm = 0
                    for kc in range(NKC):
                        dep = [(dW0s[wbuf], twl), (cD, h2_toks[kc])]
                        if kc == 0:
                            dep.append((cA, g_toks[mc - 4] if mc >= 4
                                        else e_prev))
                        tmm = ph.op("tensor", lambda eng, ps=ps, pcol=pcol,
                                    wbuf=wbuf, kc=kc: eng.matmul(
                                ps[:, pcol:pcol + 512],
                                A2[:, W0R0 + wbuf * 2048 + kc * 128:
                                   W0R0 + wbuf * 2048 + (kc + 1) * 128],
                                A2[:, H20 + kc * TC: H20 + (kc + 1) * TC],
                                start=(kc == 0), stop=(kc == NKC - 1)),
                                inc=cT, deps=dep)
                    w0mm.append(tmm)
                    g_toks.append(ph.op("scalar", lambda eng, ps=ps, pcol=pcol,
                                        mc=mc: eng.activation(
                            out=d_sb[:, mc * TC:(mc + 1) * TC],
                            in_=ps[:, pcol:pcol + 512], func=AF.Gelu,
                            bias=b0_sb[:, mc:mc + 1]), inc=cA,
                            deps=[(cT, tmm)]))
                w0_last_mm = w0mm
                w0_final = w0mm[-1]
                W1COL = (W1R0, W1R0 + 1024, 30720, 31744)
                e_ev = []
                for oh in range(2):
                    mm_last = []
                    for kc in range(64):
                        wbuf = kc % 4
                        dep_w = [(cT, mm_last[kc - 4])] if kc >= 4 else \
                            [(cT, w1_hist4[kc])]
                        if s == 0 and tc == 0 and oh == 0 and kc < 4:
                            dep_w.append((dP0S, 16 * len(p0_items)))
                        twl = dma(ph, dW1S[wbuf],
                                  A2[:, W1COL[wbuf]:W1COL[wbuf] + 1024],
                                  w1_scr_h[oh, kc], deps=dep_w)
                        tmm = 0
                        for oc in range(8):
                            ps = pA if oc < 4 else pB
                            pcol = (oc % 4) * 512
                            dep = [(dW1s[wbuf], twl), (cA, g_toks[kc])]
                            if kc == 0:
                                dep.append((cA, g_toks[-1] if oh == 0
                                            else e_ev[-1]))
                                dep.append((cD, state["stats_ev_cd"]))
                            tmm = ph.op("tensor", lambda eng, ps=ps, pcol=pcol,
                                        wbuf=wbuf, oc=oc, kc=kc: eng.matmul(
                                    ps[:, pcol:pcol + 512],
                                    A2[:, W1COL[wbuf] + oc * 128:
                                       W1COL[wbuf] + (oc + 1) * 128],
                                    d_sb[:, kc * TC:(kc + 1) * TC],
                                    start=(kc == 0), stop=(kc == 63)),
                                    inc=cT, deps=dep)
                        mm_last.append(tmm)
                    w1_hist4 = mm_last[60:64]
                    for oc in range(8):
                        ps = pA if oc < 4 else pB
                        pcol = (oc % 4) * 512
                        idx = oh * 8 + oc
                        e_ev.append(ph.op("scalar", lambda eng, ps=ps,
                                          pcol=pcol, idx=idx: eng.activation(
                                out=A2[:, E0 + idx * TC: E0 + (idx + 1) * TC],
                                in_=ps[:, pcol:pcol + 512], func=AF.Identity,
                                bias=b1_sb[:, idx:idx + 1]), inc=cA,
                                deps=[(cT, mm_last[-1]),
                                      (cD, x2_last_cd),
                                      (dOut, x2_last_dout)]))

                tw3 = ln_stats(ph, s, 3, tc,
                               lambda kc: A2[:, E0 + kc * TC:
                                             E0 + (kc + 1) * TC],
                               lambda kc: [(cA, e_ev[-1] if kc == 0
                                            else e_ev[kc])], True,
                               sev=s1evacD)
                tb31 = bcast(ph, r_b, s, 3, tc, 1,
                             deps=[(dStat, tw3), (cD, cD.n)])
                y3 = lambda par: A2f[:, (Y30 // 2) + par * 512:
                                     (Y30 // 2) + (par + 1) * 512]
                x2t = lambda par: A2f[:, (X20 // 2) + par * 512:
                                      (X20 // 2) + (par + 1) * 512]
                x2tok = [0, 0]
                touts = []
                for kc in range(NKC):
                    par = kc % 2
                    dep_z = [(dStat, tb31)]
                    if kc >= 2:
                        dep_z.append((cD, x2tok[par]))
                        dep_z.append((dOut, touts[kc - 2]))
                    tpz = dma(ph, "dPreX", x2t(par),
                              statrow_h[s, 3, tc, 2].unsqueeze(0)
                              .broadcast_to([128, TC]), deps=dep_z,
                              serial=True)
                    taz = dma(ph, "dAccX", x2t(par),
                              x1_scr_h[s, kc][:, tc * TC:(tc + 1) * TC],
                              deps=[(dPreX, tpz)], accum=True, serial=True)
                    ph.op("vector", lambda eng, kc=kc, par=par:
                          eng.tensor_mul(
                            out=y3(par),
                            in0=A2[:, E0 + kc * TC: E0 + (kc + 1) * TC],
                            in1=r_b[:]), inc=cD, deps=[(dStat, tb31)])
                    tx = ph.op("vector", lambda eng, par=par:
                               eng.tensor_add(
                            out=x2t(par), in0=x2t(par), in1=y3(par)),
                            inc=cD, deps=[(dAccX, taz)])
                    x2tok[par] = tx
                    touts.append(dma(ph, "dOut",
                        out_t[s, kc * 128:(kc + 1) * 128,
                              tc * TC:(tc + 1) * TC],
                        x2t(par), deps=[(cD, tx)], serial=True))
                x2_last_dout = touts[-1]
                e_prev = e_ev[-1]
                x2_last_cd = cD.n
            prev_toks = bar_end(ph, ENGINES)
            ph.build()

        ph = Ph(nc)
        bar_start(ph, prev_toks, ENGINES)
        if KDBG:
            dma(ph, "dTbl", dbg_o, o_scr_h.ap())
            dma(ph, "dTbl", dbg_x1, x1_scr_h.ap())
            dma(ph, "dTbl", dbg_st, statrow_h.ap())
            dma(ph, "dTbl", dbg_ns, nsum_h.ap())
        def _final_waits(eng):
            for nm, (dc, dq) in DSEMS.items():
                eng.wait_ge(dc.h, dc.n)
        ph.raw("sync", _final_waits)
        ph.build()

    return nc


def kernel(L0_attn_u, L0_attn_Wqkv, L0_attn_bqkv, L0_attn_Wo, L0_attn_bo,
           L0_attn_sinks, L0_ffn_W0, L0_ffn_bias0, L0_ffn_W1, L0_ffn_bias1):
    u_t, shared = host_prep(
        np.asarray(L0_attn_u, dtype=np.float32),
        np.asarray(L0_attn_Wqkv, dtype=np.float32),
        np.asarray(L0_attn_bqkv, dtype=np.float32),
        np.asarray(L0_attn_Wo, dtype=np.float32),
        np.asarray(L0_attn_bo, dtype=np.float32),
        np.asarray(L0_attn_sinks, dtype=np.float32),
        np.asarray(L0_ffn_W0, dtype=np.float32),
        np.asarray(L0_ffn_bias0, dtype=np.float32),
        np.asarray(L0_ffn_W1, dtype=np.float32),
        np.asarray(L0_ffn_bias1, dtype=np.float32))
    nc = build_nc()
    in_maps = []
    for c in range(NCORES):
        m = dict(shared)
        m["u_t"] = np.ascontiguousarray(u_t[c * NSEQ:(c + 1) * NSEQ])
        in_maps.append(m)
    res = run_bass_kernel_spmd(nc, in_maps, core_ids=list(range(NCORES)))
    outs = [r["out_t"] for r in res.results]
    out = np.concatenate(outs, axis=0).transpose(0, 2, 1)
    return np.ascontiguousarray(out).astype(np.float32)


if __name__ == "__main__":
    nc = build_nc()
    n = sum(1 for _ in nc.all_instructions())
    print("built ok,", n, "instructions")


# version 09
# speedup vs baseline: 1.9086x; 1.2807x over previous
import numpy as np
import ml_dtypes
from contextlib import ExitStack

import concourse.bass as bass
import concourse.mybir as mybir
from concourse.bass_utils import run_bass_kernel_spmd

F32 = mybir.dt.float32
BF16 = mybir.dt.bfloat16
AF = mybir.ActivationFunctionType
ALU = mybir.AluOpType
BFNP = ml_dtypes.bfloat16

NCORES = 8
NSEQ = 2
L = 2048
DM = 2048
NH = 32
NHG = 8
HPG = 4
TC = 512
NTC = L // TC
NKC = DM // 128
CEXP = 12.0
EPS = 1e-5

ENGINES = ("sync", "tensor", "vector", "scalar", "gpsimd")
import os
BUILD_PHASES = os.environ.get("KBUILD", "ABCD")
KD = int(os.environ.get("KD", "5"))


class Ctr:
    __slots__ = ("h", "n", "engine")

    def __init__(self, h, engine=None):
        self.h = h
        self.n = 0
        self.engine = engine


class Ph:
    def __init__(self, nc):
        self.nc = nc
        self.streams = {e: [] for e in ENGINES}
        self.waited = {e: {} for e in ENGINES}

    def wait(self, e, deps):
        todo = []
        for c, v in deps:
            if v <= 0 or c.engine == e:
                continue
            if self.waited[e].get(id(c), 0) >= v:
                continue
            self.waited[e][id(c)] = v
            todo.append((c.h, v))
        if todo:
            def f(eng, todo=todo):
                for h, v in todo:
                    eng.wait_ge(h, v)
            self.streams[e].append(f)

    def op(self, e, fn, inc, k=1, deps=()):
        self.wait(e, deps)
        inc.n += k
        def g(eng, fn=fn, h=inc.h, k=k):
            fn(eng).then_inc(h, k)
        self.streams[e].append(g)
        return inc.n

    def raw(self, e, fn):
        self.streams[e].append(fn)

    def build(self):
        with self.nc.Block() as blk:
            for e in ENGINES:
                fns = self.streams[e]
                if fns:
                    getattr(blk, e)(lambda eng, fns=fns: [f(eng) for f in fns])


def _band_mask(t):
    q0 = t * 128
    m = np.zeros((384, 128), dtype=np.float32)
    m[0:16] = 1.0
    kpos = q0 - 64 + np.arange(320)
    qpos = q0 + np.arange(128)
    band = (np.abs(kpos[:, None] - qpos[None, :]) <= 64) \
        & (kpos[:, None] >= 0) & (kpos[:, None] < L)
    m[64:384] = band.astype(np.float32)
    return m


def host_prep(u, Wqkv, bqkv, Wo, bo, sinks, W0, b0, W1, b1):
    d = {}
    u_t = np.ascontiguousarray(u.transpose(0, 2, 1))
    cols = []
    for hg in range(NHG):
        hs = [HPG * hg + j for j in range(HPG)]
        qA = [64 * h + dd for h in hs for dd in range(32)]
        qB = [64 * h + 32 + dd for h in hs for dd in range(32)]
        kA = [2048 + c for c in qA]
        kB = [2048 + c for c in qB]
        vv = [4096 + 64 * h + dd for h in hs for dd in range(64)]
        cols += qA + qB + kA + kB + vv
    cols = np.array(cols)
    d["wqkv"] = np.ascontiguousarray(Wqkv.T[:, cols])
    d["bqkv_pm"] = np.ascontiguousarray(bqkv[cols].reshape(48, 128).T)
    vrow = np.concatenate([bqkv[cols[hg * 768 + 512: hg * 768 + 768]]
                           for hg in range(NHG)])
    d["bqkv_vrow"] = np.ascontiguousarray(vrow[None, :])
    sa = np.zeros((128, NHG * 64), dtype=np.float32)
    sbv = np.zeros((128, NHG * 64), dtype=np.float32)
    spv = np.zeros((64, NHG * 260), dtype=np.float32)
    for hg in range(NHG):
        for j in range(HPG):
            h = HPG * hg + j
            sa[32 * j:32 * j + 32, hg * 64: hg * 64 + 16] = sinks[h, :, 0:32].T
            sbv[32 * j:32 * j + 32, hg * 64: hg * 64 + 16] = sinks[h, :, 32:64].T
            spv[0:16, hg * 260 + 65 * j: hg * 260 + 65 * j + 64] = sinks[h]
            spv[:, hg * 260 + 65 * j + 64] = 1.0
    d["sinks_a"] = sa
    d["sinks_b"] = sbv
    d["sinks_pv"] = spv
    d["wo_blk"] = np.ascontiguousarray(
        Wo.T.reshape(16, 128, 16, 128).transpose(2, 1, 0, 3)
        .reshape(16, 128, 2048))
    d["w0_blk"] = np.ascontiguousarray(
        W0.T.reshape(16, 128, 64, 128).transpose(2, 1, 0, 3)
        .reshape(64, 128, 2048))
    d["w1_blk"] = np.ascontiguousarray(
        W1.T.reshape(64, 128, 2, 1024).transpose(2, 0, 1, 3))
    d["bo_pm"] = np.ascontiguousarray(bo.reshape(16, 128).T)
    d["b0_pm"] = np.ascontiguousarray(b0.reshape(64, 128).T)
    d["b1_pm"] = np.ascontiguousarray(b1.reshape(16, 128).T)
    half = 32
    inv = 1.0 / (10000.0 ** (np.arange(half, dtype=np.float64) / half))
    ang = np.arange(L, dtype=np.float64)[:, None] * inv[None, :]
    cosT = np.tile(np.cos(ang).T, (4, 1)).astype(np.float32)
    sinT = np.tile(np.sin(ang).T, (4, 1)).astype(np.float32)
    d["rope_cq"] = (0.125 * cosT).astype(BFNP)
    d["rope_sq"] = (0.125 * sinT).astype(BFNP)
    d["rope_ck"] = cosT.astype(BFNP)
    d["rope_sk"] = sinT.astype(BFNP)
    mk = np.zeros((3, 128, 384), dtype=np.float32)
    for vi, t in enumerate([0, 1, 15]):
        m = _band_mask(t)
        for c in range(3):
            mk[vi, :, c * 128:(c + 1) * 128] = m[c * 128:(c + 1) * 128, :]
    d["mask3"] = ((mk - 1.0) * 1e9).astype(BFNP)
    d["id128"] = np.eye(128, dtype=np.float32).astype(BFNP)
    return u_t, d


def build_nc():
    nc = bass.Bass("TRN2", target_bir_lowering=False,
                   detect_race_conditions=(
                       os.environ.get("KRACE", "1") == "1"))

    u_t = nc.dram_tensor("u_t", [NSEQ, DM, L], F32, kind="ExternalInput").ap()
    wqkv = nc.dram_tensor("wqkv", [DM, 6144], F32, kind="ExternalInput").ap()
    bqkv_pm = nc.dram_tensor("bqkv_pm", [128, 48], F32, kind="ExternalInput").ap()
    bqkv_vrow = nc.dram_tensor("bqkv_vrow", [1, 2048], F32, kind="ExternalInput").ap()
    sinks_a = nc.dram_tensor("sinks_a", [128, 512], F32, kind="ExternalInput").ap()
    sinks_b = nc.dram_tensor("sinks_b", [128, 512], F32, kind="ExternalInput").ap()
    sinks_pv = nc.dram_tensor("sinks_pv", [64, 2080], F32, kind="ExternalInput").ap()
    wo_blk = nc.dram_tensor("wo_blk", [16, 128, 2048], F32, kind="ExternalInput").ap()
    w0_blk = nc.dram_tensor("w0_blk", [64, 128, 2048], F32, kind="ExternalInput").ap()
    w1_blk = nc.dram_tensor("w1_blk", [2, 64, 128, 1024], F32, kind="ExternalInput").ap()
    bo_pm = nc.dram_tensor("bo_pm", [128, 16], F32, kind="ExternalInput").ap()
    b0_pm = nc.dram_tensor("b0_pm", [128, 64], F32, kind="ExternalInput").ap()
    b1_pm = nc.dram_tensor("b1_pm", [128, 16], F32, kind="ExternalInput").ap()
    rope_cq = nc.dram_tensor("rope_cq", [128, L], BF16, kind="ExternalInput").ap()
    rope_sq = nc.dram_tensor("rope_sq", [128, L], BF16, kind="ExternalInput").ap()
    rope_ck = nc.dram_tensor("rope_ck", [128, L], BF16, kind="ExternalInput").ap()
    rope_sk = nc.dram_tensor("rope_sk", [128, L], BF16, kind="ExternalInput").ap()
    mask3d = nc.dram_tensor("mask3", [3, 128, 384], BF16, kind="ExternalInput").ap()
    id128d = nc.dram_tensor("id128", [128, 128], BF16, kind="ExternalInput").ap()
    out_t = nc.dram_tensor("out_t", [NSEQ, DM, L], F32, kind="ExternalOutput").ap()
    KDBG = os.environ.get("KDBG", "0") == "1"
    if KDBG:
        dbg_o = nc.dram_tensor("dbg_o", [NSEQ, NH, 64, L], BF16,
                               kind="ExternalOutput").ap()
        dbg_x1 = nc.dram_tensor("dbg_x1", [NSEQ, NKC, 128, L], F32,
                                kind="ExternalOutput").ap()
        dbg_st = nc.dram_tensor("dbg_st", [NSEQ, 4, NTC, 3, TC], F32,
                                kind="ExternalOutput").ap()
        dbg_ns = nc.dram_tensor("dbg_ns", [2, 4096], BF16,
                                kind="ExternalOutput").ap()

    wqkv_scr_h = nc.dram_tensor("wqkv_scr", [8, 128, 12288], BF16, kind="Internal")
    wo_scr_h = nc.dram_tensor("wo_scr", [16, 128, 2048], BF16, kind="Internal")
    w0_scr_h = nc.dram_tensor("w0_scr", [64, 128, 2048], BF16, kind="Internal")
    w1_scr_h = nc.dram_tensor("w1_scr", [2, 64, 128, 1024], BF16, kind="Internal")
    o_scr_h = nc.dram_tensor("o_scr", [NSEQ, NH, 64, L], BF16, kind="Internal")
    x1_scr_h = nc.dram_tensor("x1_scr", [NSEQ, NKC, 128, L], F32, kind="Internal")
    statrow_h = nc.dram_tensor("statrow", [NSEQ, 4, NTC, 3, TC], F32, kind="Internal")
    nsum_h = nc.dram_tensor("nsum_scr", [2, 4096], BF16, kind="Internal")
    rrow_h = nc.dram_tensor("rrow_scr", [NSEQ, NH, 2048], BF16, kind="Internal")

    S = ExitStack()
    with S:
        cT = Ctr(S.enter_context(nc.semaphore("cT")), "tensor")
        cA = Ctr(S.enter_context(nc.semaphore("cA")), "scalar")
        cD = Ctr(S.enter_context(nc.semaphore("cD")), "vector")
        cP = Ctr(S.enter_context(nc.semaphore("cP")), "gpsimd")
        cS = Ctr(S.enter_context(nc.semaphore("cS")), "sync")
        ECTR = {"sync": cS, "tensor": cT, "vector": cD, "scalar": cA,
                "gpsimd": cP}
        DSEMS = {}
        def dsem(name, q):
            c = Ctr(S.enter_context(nc.semaphore(name)))
            c.engine = None
            DSEMS[name] = (c, q)
            return c
        dU = dsem("dU", "sync")
        dSlab = dsem("dSlab", "sync")
        dTbl = dsem("dTbl", "sync")
        dOtc = dsem("dOtc", "sync")
        dWos = [dsem(f"dWos{j}", "sync") for j in range(4)]
        dW0s = [dsem(f"dW0s{j}", "sync") for j in range(4)]
        dW1s = [dsem(f"dW1s{j}", "sync") for j in range(4)]
        dWOS = [(c, "sync") for c in dWos]
        dW0S = [(c, "sync") for c in dW0s]
        dW1S = [(c, "sync") for c in dW1s]
        dX1w = dsem("dX1w", "gpsimd")
        dOw = dsem("dOw", "sync")
        dStat = dsem("dStat", "scalar")
        dNorm = dsem("dNorm", "scalar")
        dPre = dsem("dPre", "scalar")
        dPreX = dsem("dPreX", "scalar")
        dOut = dsem("dOut", "scalar")
        dAcc = dsem("dAcc", "gpsimd")
        dAccX = dsem("dAccX", "gpsimd")
        P0D = 4
        dP0Ls = [dsem(f"dP0L{j}", "gpsimd") for j in range(P0D)]
        dP0Ss = [dsem(f"dP0S{j}", "gpsimd") for j in range(P0D)]

        def sbt(name, shape, dt):
            return S.enter_context(nc.sbuf_tensor(name, shape, dt))

        pA = S.enter_context(nc.psum_tensor("pA", [128, 2048], F32))
        pB = S.enter_context(nc.psum_tensor("pB", [128, 2048], F32))

        A1 = sbt("A1", [128, 32768], BF16)
        A2 = sbt("A2", [128, 32768], BF16)
        A3b = sbt("A3", [128, 16384], BF16)
        A2f = A2.bitcast(F32)
        A3f = A3b.bitcast(F32)
        s1evac = A2f[0:1, 15360:16384]
        A1f = A1.bitcast(F32)
        s1evacC = A1f[0:1, 8192:9216]
        s1evacD = A2f[0:1, 14336:15360]

        bqkv_sb = sbt("bqkvt", [128, 48], F32)
        bo_sb = sbt("bot", [128, 16], F32)
        b0_sb = sbt("b0t", [128, 64], F32)
        b1_sb = sbt("b1t", [128, 16], F32)
        sA_sb = sbt("sAt", [128, 512], BF16)
        sB_sb = sbt("sBt", [128, 512], BF16)
        spv_sb = sbt("spvt", [64, 2080], BF16)
        bvrow_sb = sbt("bvrowt", [1, 2048], BF16)
        eps_t = sbt("epsT", [128, 1], F32)
        cexp_t = sbt("cexpT", [128, 1], F32)
        ones_c = sbt("onesC", [128, 1], BF16)
        ones_r = sbt("onesR", [1, 128], BF16)
        srows = sbt("srows", [1, 1536], F32)
        rc_in = sbt("rcin", [128, 64], BF16)
        rc_out = sbt("rcout", [128, 64], BF16)
        mnr_b = sbt("mnrb", [128, 2 * TC], F32)
        mneg_b = mnr_b[:, 0:TC]
        r_b = mnr_b[:, TC:2 * TC]
        r3_b = sbt("r3b", [128, TC], F32)
        id_sb = sbt("id128t", [128, 128], BF16)
        xsq_sb = sbt("xsq", [128, 1024], BF16)
        p0f = sbt("p0f", [128, 768 * P0D], F32)
        p0b = sbt("p0b", [128, 768 * P0D], BF16)

        h_sb = A1
        d_sb = A1
        u_sb = A3f
        xbfA = lambda: A2
        SLAB0 = 0
        QA0, QB0, KA0, KB0 = 12288, 14336, 16384, 18432
        ABF0, BBF0, VEXT0, PT0 = 20480, 22528, 24576, 28736
        OPAIR0, RNB0, MSK0 = 0, 4096, 6144
        CQ0, SQ0, CK0, SK0 = 7296, 9344, 11392, 13440
        OTC0, WOR0, XBFC0 = 0, 8192, 16384
        E0, H20, W0R0, W1R0, Y30, X20 = 0, 8192, 16384, 24576, 26624, 28672

        def bar_end(ph, engines):
            toks = {}
            for e in engines:
                c = ECTR[e]
                for nm, (dc, dq) in DSEMS.items():
                    if dq == e:
                        ph.wait(e, [(dc, dc.n)])
                def f(eng, c=c):
                    eng.drain()
                    eng.sem_inc(c.h, 1)
                c.n += 1
                ph.raw(e, f)
                toks[e] = c.n
            return toks

        def bar_start(ph, toks, engines):
            for e in engines:
                ph.wait(e, [(ECTR[e2], toks[e2]) for e2 in toks if e2 != e])

        def dma(ph, sem, out, in_, deps=(), accum=False, serial=False):
            ctr, q = DSEMS[sem] if isinstance(sem, str) else sem
            kw = {"accum_op": ALU.add} if accum else {}
            if serial:
                deps = list(deps) + [(ctr, ctr.n)]
            def f(eng, out=out, in_=in_, kw=kw):
                return eng.dma_start(out=out, in_=in_, **kw)
            return ph.op(q, f, inc=ctr, k=16, deps=deps)

        state = {"statrow_dma": 0, "stats_ev_cd": 0}

        KST = int(os.environ.get("KST", "9"))
        def ln_stats(ph, s, site, tc, xap, chunk_deps, want_mnegr, gate=False,
                     sev=None):
            sev = s1evac if sev is None else sev
            s2_toks = []
            for kc in range(NKC):
                dep = list(chunk_deps(kc))
                if kc == 0:
                    dep.append((cD, state["stats_ev_cd"]))
                if kc >= 2:
                    dep.append((cT, s2_toks[kc - 2]))
                tk = ph.op("vector", lambda eng, kc=kc: eng.tensor_mul(
                    out=xsq_sb[:, (kc % 2) * 512:(kc % 2 + 1) * 512],
                    in0=xap(kc), in1=xap(kc)), inc=cD, deps=dep)
                ph.op("tensor", lambda eng, kc=kc: eng.matmul(
                    pA[0:1, 1024:1536], ones_c[:], xap(kc),
                    start=(kc == 0), stop=(kc == NKC - 1)),
                    inc=cT, deps=dep)
                t2 = ph.op("tensor", lambda eng, kc=kc: eng.matmul(
                    pA[0:1, 1536:2048], ones_c[:],
                    xsq_sb[:, (kc % 2) * 512:(kc % 2 + 1) * 512],
                    start=(kc == 0), stop=(kc == NKC - 1)),
                    inc=cT, deps=[(cD, tk)])
                s2_toks.append(t2)
            if gate and KST < 2:
                return state["statrow_dma"]
            tev = ph.op("vector", lambda eng: eng.tensor_copy(
                out=sev, in_=pA[0:1, 1024:2048]), inc=cD,
                deps=[(cT, s2_toks[-1]), (dStat, state["statrow_dma"])])
            state["stats_ev_cd"] = tev
            t_mneg = ph.op("vector", lambda eng: eng.tensor_scalar_mul(
                out=srows[0:1, 0:512], in0=sev[0:1, 0:512],
                scalar1=-1.0 / DM), inc=cD)
            ph.op("vector", lambda eng: eng.tensor_mul(
                out=srows[0:1, 1024:1536], in0=srows[0:1, 0:512],
                in1=srows[0:1, 0:512]), inc=cD)
            t_v = ph.op("vector", lambda eng: eng.scalar_tensor_tensor(
                out=sev[0:1, 512:1024], in0=sev[0:1, 512:1024],
                scalar=1.0 / DM, in1=srows[0:1, 1024:1536],
                op0=ALU.mult, op1=ALU.subtract), inc=cD)
            if gate and KST < 3:
                return state["statrow_dma"]
            ph.op("scalar", lambda eng: eng.activation(
                out=srows[0:1, 1024:1536], in_=sev[0:1, 512:1024],
                func=AF.Ln, bias=eps_t[0:1, :]), inc=cA, deps=[(cD, t_v)])
            t_r = ph.op("scalar", lambda eng: eng.activation(
                out=srows[0:1, 512:1024], in_=srows[0:1, 1024:1536],
                func=AF.Exp, scale=-0.5), inc=cA)
            deps_w = [(cA, t_r), (cD, t_mneg)]
            if want_mnegr:
                t_mr = ph.op("vector", lambda eng: eng.tensor_mul(
                    out=srows[0:1, 1024:1536], in0=srows[0:1, 0:512],
                    in1=srows[0:1, 512:1024]), inc=cD, deps=[(cA, t_r)])
                deps_w = [(cD, t_mr)]
            if gate and KST < 4:
                return state["statrow_dma"]
            tw = dma(ph, "dStat",
                     statrow_h[s, site, tc].rearrange("a c -> (a c)").unsqueeze(0),
                     srows[0:1, 0:1536], deps=deps_w, serial=True)
            state["statrow_dma"] = tw
            return tw

        def bcast(ph, dst, s, site, tc, kind, deps, n=1):
            src = statrow_h[s, site, tc].rearrange("a c -> (a c)")[
                kind * TC:(kind + n) * TC].unsqueeze(0).broadcast_to(
                [128, n * TC])
            return dma(ph, "dStat", dst, src, deps=deps, serial=True)

        NB = [e for e in ENGINES if e != "gpsimd"]

        ph = Ph(nc)
        for dst, src in [(bqkv_sb[:], bqkv_pm), (bo_sb[:], bo_pm),
                         (b0_sb[:], b0_pm), (b1_sb[:], b1_pm)]:
            dma(ph, "dStat", dst, src, serial=True)
        t1 = dma(ph, "dStat", A2f[:, 0:512], sinks_a, serial=True)
        ph.op("vector", lambda eng: eng.tensor_copy(
            out=sA_sb[:], in_=A2f[:, 0:512]), inc=cD, deps=[(dStat, t1)])
        t2 = dma(ph, "dStat", A2f[:, 512:1024], sinks_b, serial=True)
        ph.op("vector", lambda eng: eng.tensor_copy(
            out=sB_sb[:], in_=A2f[:, 512:1024]), inc=cD, deps=[(dStat, t2)])
        t3 = dma(ph, "dStat", A2f[0:64, 1024:3104], sinks_pv, serial=True)
        ph.op("vector", lambda eng: eng.tensor_copy(
            out=spv_sb[:], in_=A2f[0:64, 1024:3104]), inc=cD, deps=[(dStat, t3)])
        t5 = dma(ph, "dStat", A2f[0:1, 3104:5152], bqkv_vrow, serial=True)
        ph.op("vector", lambda eng: eng.tensor_copy(
            out=bvrow_sb[:], in_=A2f[0:1, 3104:5152]), inc=cD, deps=[(dStat, t5)])
        dma(ph, "dStat", id_sb[:], id128d, serial=True)
        ph.op("vector", lambda eng: eng.memset(eps_t[:], EPS), inc=cD)
        ph.op("vector", lambda eng: eng.memset(cexp_t[:], -CEXP), inc=cD)
        ph.op("vector", lambda eng: eng.memset(ones_c[:], 1.0), inc=cD)
        ph.op("vector", lambda eng: eng.memset(ones_r[:], 1.0), inc=cD)
        prev_toks = bar_end(ph, ENGINES)
        ph.build()

        p0_items = []
        for hg in range(NHG):
            for kc in range(NKC):
                p0_items.append((wqkv[kc * 128:(kc + 1) * 128,
                                      hg * 768:(hg + 1) * 768],
                                 wqkv_scr_h[hg][:, kc * 768:(kc + 1) * 768]))
        for mc in range(16):
            for pc in range(4):
                p0_items.append((wo_blk[mc][:, pc * 512:(pc + 1) * 512],
                                 wo_scr_h[mc][:, pc * 512:(pc + 1) * 512]))
        for mc in range(64):
            for pc in range(4):
                p0_items.append((w0_blk[mc][:, pc * 512:(pc + 1) * 512],
                                 w0_scr_h[mc][:, pc * 512:(pc + 1) * 512]))
        for oh in range(2):
            for kc in range(64):
                for pc in range(2):
                    p0_items.append(
                        (w1_blk[oh, kc][:, pc * 512:(pc + 1) * 512],
                         w1_scr_h[oh, kc][:, pc * 512:(pc + 1) * 512]))
        p0 = {"li": 0, "ci": 0, "stok": [0] * len(p0_items),
              "conv": [0] * len(p0_items),
              "ltok": [0] * len(p0_items),
              "sdep": [None] * len(p0_items)}

        def p0_sdep(i0):
            return [(c, c.n) for c in dP0Ss]

        def p0_pump(ph, n):
            N = len(p0_items)
            def load_one(i):
                src, dst = p0_items[i]
                j = i % P0D
                o0 = j * 768
                w = src.shape[1]
                p0["ltok"][i] = dma(ph, (dP0Ls[j], "gpsimd"),
                                    p0f[:, o0:o0 + w], src)
            while p0["li"] < N and p0["li"] < p0["ci"] + P0D:
                load_one(p0["li"])
                p0["li"] += 1
            for _ in range(n):
                i = p0["ci"]
                if i >= N:
                    return
                p0["ci"] += 1
                src, dst = p0_items[i]
                j = i % P0D
                o0 = j * 768
                w = src.shape[1]
                deps_c = [(dP0Ls[j], p0["ltok"][i])]
                if i >= P0D:
                    deps_c.append((dP0Ss[j], p0["stok"][i - P0D]))
                tcv = ph.op("gpsimd", lambda eng, o0=o0, w=w:
                            eng.tensor_copy(out=p0b[:, o0:o0 + w],
                                            in_=p0f[:, o0:o0 + w]),
                            inc=cP, deps=deps_c)
                ts = dma(ph, (dP0Ss[j], "gpsimd"), dst, p0b[:, o0:o0 + w])
                p0["conv"][i] = tcv
                p0["stok"][i] = ts
                p0["sdep"][i] = tuple((dP0Ss[j2], dP0Ss[j2].n)
                                      for j2 in range(P0D))
                if p0["li"] < N:
                    load_one(p0["li"])
                    p0["li"] += 1

        x1stat = {}

        nseq_build = NSEEQ if False else (NSEQ if "D" in BUILD_PHASES or True else 1)
        nseq_build = int(os.environ.get("KSEQ", str(NSEQ)))
        for s in range(nseq_build):
            if "A" not in BUILD_PHASES:
                continue
            ph = Ph(nc)
            bar_start(ph, prev_toks, NB)
            prev_cd = 0
            prev_ct = 0
            for tc in range(NTC):
                ld = []
                for kc in range(NKC):
                    dep = [(cD, prev_cd)] if kc == 0 else []
                    ld.append(dma(ph, "dU", u_sb[:, kc * TC:(kc + 1) * TC],
                                  u_t[s, kc * 128:(kc + 1) * 128,
                                      tc * TC:(tc + 1) * TC], deps=dep))
                cps = []
                for kc in range(NKC):
                    dep = [(dU, ld[-1])]
                    if kc == 0:
                        dep.append((cT, prev_ct))
                    cps.append(ph.op("scalar", lambda eng, kc=kc:
                                     eng.activation(
                            out=A2[:, kc * TC:(kc + 1) * TC],
                            in_=u_sb[:, kc * TC:(kc + 1) * TC], func=AF.Copy),
                            inc=cA, deps=dep))
                tw = ln_stats(ph, s, 0, tc,
                              lambda kc: A2[:, kc * TC:(kc + 1) * TC],
                              lambda kc: [(cA, cps[kc])], False)
                prev_ct = cT.n
                tb1 = bcast(ph, mnr_b[:], s, 0, tc, 0,
                            deps=[(dStat, tw), (cD, cD.n)], n=2)
                tb2 = tb1
                for kc in range(NKC):
                    ph.op("vector", lambda eng, kc=kc: eng.tensor_add(
                        out=u_sb[:, kc * TC:(kc + 1) * TC],
                        in0=u_sb[:, kc * TC:(kc + 1) * TC], in1=mneg_b[:]),
                        inc=cD, deps=[(dStat, tb1)])
                    ph.op("vector", lambda eng, kc=kc, tc=tc: eng.tensor_mul(
                        out=h_sb[:, kc * L + tc * TC: kc * L + (tc + 1) * TC],
                        in0=u_sb[:, kc * TC:(kc + 1) * TC], in1=r_b[:]),
                        inc=cD, deps=[(dStat, tb2)])
                prev_cd = cD.n
                if s == 0:
                    p0_pump(ph, 40)
            prev_toks = bar_end(ph, NB)
            ph.build()

            if "B" not in BUILD_PHASES:
                continue
            ph = Ph(nc)
            bar_start(ph, prev_toks, NB)
            tbl_toks = []
            for off, src in [(CQ0, rope_cq), (SQ0, rope_sq),
                             (CK0, rope_ck), (SK0, rope_sk)]:
                tbl_toks.append(dma(ph, "dTbl", A3b[:, off:off + 2048], src))
            tmsk = dma(ph, "dTbl",
                       A3b[:, MSK0:MSK0 + 1152].rearrange(
                           "p (v c) -> p v c", v=3),
                       mask3d.transpose([1, 0, 2]))
            tbl_all = tmsk
            slab_cons = 0
            exp_tok = [0, 0]
            pv_tok = [0, 0]
            oev_tok = [0, 0]
            att_done_ct = 0
            last_oscr = 0
            opair = A3b[0:65, OPAIR0:OPAIR0 + 4096]

            def tabap(off):
                return A3b[:, off:off + 2048]

            for hg in range(NHG):
                depl = [(cT, slab_cons)]
                if s == 0:
                    depl += p0_sdep(hg * 16 + 15)
                tslab = dma(ph, "dSlab", A2[:, SLAB0:SLAB0 + 12288],
                            wqkv_scr_h[hg], deps=depl)
                rope_final = {}
                rope_abf = {}
                evk = {}
                for mi, (dA0, dB0, tC0, tS0) in enumerate(
                        [(QA0, QB0, CQ0, SQ0), (KA0, KB0, CK0, SK0)]):
                    dsta = A2[:, dA0:dA0 + 2048]
                    dstb = A2[:, dB0:dB0 + 2048]
                    tabC = tabap(tC0)
                    tabS = tabap(tS0)
                    abf = A2[:, ABF0:ABF0 + 2048]
                    bbf = A2[:, BBF0:BBF0 + 2048]
                    mm_last = {}
                    for half in range(2):
                        mci = mi * 2 + half
                        ps = pA if half == 0 else pB
                        for tcq in range(NTC):
                            for kc in range(NKC):
                                dep = [(dSlab, tslab)]
                                if tcq == 0 and kc == 0:
                                    if mi == 0:
                                        dep += [(cA, exp_tok[0]),
                                                (cA, exp_tok[1]),
                                                (cD, oev_tok[0]),
                                                (cD, oev_tok[1]),
                                                (cT, pv_tok[0]),
                                                (cT, pv_tok[1])]
                                    else:
                                        dep += [(cA, evk[(0, half)]),
                                                (cD, rope_abf[0])]
                                ph.op("tensor", lambda eng, ps=ps, tcq=tcq,
                                      kc=kc, mci=mci: eng.matmul(
                                        ps[:, tcq * 512:(tcq + 1) * 512],
                                        A2[:, SLAB0 + kc * 768 + mci * 128:
                                           SLAB0 + kc * 768 + (mci + 1) * 128],
                                        h_sb[:, kc * L + tcq * TC:
                                             kc * L + (tcq + 1) * TC],
                                        start=(kc == 0), stop=(kc == NKC - 1)),
                                      inc=cT, deps=dep)
                        mm_last[half] = cT.n
                    bca = bqkv_sb[:, hg * 6 + mi * 2: hg * 6 + mi * 2 + 1]
                    bcb = bqkv_sb[:, hg * 6 + mi * 2 + 1: hg * 6 + mi * 2 + 2]
                    depe = [(cT, mm_last[0])]
                    depe2 = [(cT, mm_last[1])]
                    if mi == 1:
                        depe.append((cD, rope_abf[0]))
                        depe2.append((cD, rope_abf[0]))
                    te1 = ph.op("scalar", lambda eng, bca=bca, abf=abf:
                                eng.activation(out=abf, in_=pA[:],
                                               func=AF.Identity, bias=bca),
                                inc=cA, deps=depe)
                    te2 = ph.op("scalar", lambda eng, bcb=bcb, bbf=bbf:
                                eng.activation(out=bbf, in_=pB[:],
                                               func=AF.Identity, bias=bcb),
                                inc=cA, deps=depe2)
                    evk[(mi, 0)] = te1
                    evk[(mi, 1)] = te2
                    d1 = [(cA, te1), (cA, te2), (dTbl, tbl_all)]
                    ph.op("vector", lambda eng, dsta=dsta, abf=abf, tabC=tabC:
                          eng.tensor_mul(out=dsta, in0=abf, in1=tabC),
                          inc=cD, deps=d1)
                    ph.op("vector", lambda eng, dstb=dstb, bbf=bbf, tabS=tabS:
                          eng.tensor_mul(out=dstb, in0=bbf, in1=tabS), inc=cD)
                    ph.op("vector", lambda eng, dsta=dsta, dstb=dstb:
                          eng.tensor_sub(out=dsta, in0=dsta, in1=dstb), inc=cD)
                    ta4 = ph.op("vector", lambda eng, abf=abf, tabS=tabS:
                                eng.tensor_mul(out=abf, in0=abf, in1=tabS),
                                inc=cD)
                    ph.op("vector", lambda eng, dstb=dstb, bbf=bbf, tabC=tabC:
                          eng.tensor_mul(out=dstb, in0=bbf, in1=tabC), inc=cD)
                    trp = ph.op("vector", lambda eng, dstb=dstb, abf=abf:
                                eng.tensor_add(out=dstb, in0=abf, in1=dstb),
                                inc=cD)
                    rope_abf[mi] = trp
                    rope_final[mi] = trp
                vext0 = VEXT0
                tms = ph.op("vector", lambda eng: eng.memset(
                    A2[:, vext0:vext0 + 4160], 1.0), inc=cD,
                    deps=[(cT, att_done_ct)])
                vevac = []
                for tb in range(16):
                    slot = tb % 8
                    ps = pA if slot < 4 else pB
                    col = (slot % 4) * 512
                    dep0 = [(cA, evk[(1, 0)]), (cA, evk[(1, 1)]),
                            (cD, rope_abf[1])]
                    if tb >= 8:
                        dep0.append((cA, vevac[tb - 8]))
                    ph.op("tensor", lambda eng, ps=ps, col=col, hg=hg:
                          eng.matmul(ps[:, col:col + 256], ones_r[:],
                                     bvrow_sb[0:1, hg * 256:(hg + 1) * 256],
                                     start=True, stop=False),
                          inc=cT, deps=dep0)
                    tbm = 0
                    for kc in range(NKC):
                        tbm = ph.op("tensor", lambda eng, ps=ps, col=col,
                                    kc=kc, tb=tb: eng.matmul(
                                ps[:, col:col + 256],
                                h_sb[:, kc * L + tb * 128:
                                     kc * L + tb * 128 + 128],
                                A2[:, SLAB0 + kc * 768 + 512:
                                   SLAB0 + kc * 768 + 768],
                                start=False, stop=(kc == NKC - 1)),
                            inc=cT, deps=[(dSlab, tslab)])
                    tve = ph.op("scalar", lambda eng, ps=ps, col=col, tb=tb:
                                eng.activation(
                            out=A2[:, vext0 + tb * 260:
                                   vext0 + tb * 260 + 260].rearrange(
                                "p (j dd) -> p j dd", j=4)[:, :, 0:64],
                            in_=ps[:, col:col + 256].rearrange(
                                "p (j dd) -> p j dd", j=4),
                            func=AF.Copy), inc=cA, deps=[(cT, tbm), (cD, tms)])
                    vevac.append(tve)
                slab_cons = cT.n
                for pp in range(2):
                    for t in range(16):
                        slot = t % 2
                        sbase = slot * 1024
                        q0 = t * 128
                        mt = 0 if t == 0 else (2 if t == 15 else 1)
                        dep_first = [(cD, rope_final[0]), (cD, rope_final[1]),
                                     (cA, exp_tok[slot]), (cA, vevac[15])]
                        first = True
                        for jj in range(2):
                            j = pp * 2 + jj
                            b = 32 * j
                            base = sbase + jj * 512
                            w0s = q0 - 64 if t > 0 else 0
                            w2s = q0 + 128 if t < 15 else q0
                            ka = A2[:, KA0:KA0 + 2048]
                            kb = A2[:, KB0:KB0 + 2048]
                            groups = [
                                (sA_sb[b:b + 32, hg * 64: hg * 64 + 64],
                                 sB_sb[b:b + 32, hg * 64: hg * 64 + 64],
                                 pA[0:64, base:base + 128], (b, 0)),
                                (ka[b:b + 32, w0s:w0s + 64],
                                 kb[b:b + 32, w0s:w0s + 64],
                                 pA[64:128, base:base + 128], (b, 64)),
                                (ka[b:b + 32, q0:q0 + 128],
                                 kb[b:b + 32, q0:q0 + 128],
                                 pA[0:128, base + 128:base + 256], (b, 0)),
                                (ka[b:b + 32, w2s:w2s + 128],
                                 kb[b:b + 32, w2s:w2s + 128],
                                 pA[0:128, base + 256:base + 384], (b, 0)),
                            ]
                            qaap = A2[b:b + 32, QA0 + q0:QA0 + q0 + 128]
                            qbap = A2[b:b + 32, QB0 + q0:QB0 + q0 + 128]
                            mregs = ((0, 64, 0), (64, 128, 0),
                                     (0, 128, 128), (0, 128, 256))
                            for gi, (lA, lB, outap, tp) in enumerate(groups):
                                mp0, mp1, rc = mregs[gi]
                                mc = MSK0 + mt * 384 + rc
                                ph.op("tensor", lambda eng, base=base, mc=mc,
                                      mp0=mp0, mp1=mp1, rc=rc: eng.matmul(
                                        pA[mp0:mp1, base + rc:base + rc + 128],
                                        id_sb[mp0:mp1, mp0:mp1],
                                        A3b[mp0:mp1, mc:mc + 128],
                                        start=True, stop=False,
                                        tile_position=(mp0, mp0)),
                                      inc=cT,
                                      deps=(dep_first + [(dTbl, tbl_all)])
                                      if first else [])
                                first = False
                                ph.op("tensor", lambda eng, outap=outap, lA=lA,
                                      qaap=qaap, tp=tp: eng.matmul(
                                        outap, lA, qaap, start=False,
                                        stop=False, tile_position=tp),
                                      inc=cT)
                                ph.op("tensor", lambda eng, outap=outap, lB=lB,
                                      qbap=qbap, tp=tp: eng.matmul(
                                        outap, lB, qbap, start=False,
                                        stop=True, tile_position=tp), inc=cT)
                        tmm_last = cT.n
                        texp = ph.op("scalar", lambda eng, sbase=sbase,
                                     slot=slot: eng.activation(
                                out=A2[:, PT0 + slot * 768:
                                       PT0 + (slot + 1) * 768].rearrange(
                                    "p (hh c) -> p hh c", hh=2),
                                in_=pA[:, sbase:sbase + 1024].rearrange(
                                    "p (hh c) -> p hh c", hh=2)[:, :, 0:384],
                                func=AF.Exp, bias=cexp_t[:]), inc=cA,
                                deps=[(cT, tmm_last), (cT, pv_tok[slot])])
                        exp_tok[slot] = texp
                        obank = (t // 2) % 2
                        ocol = obank * 512 + (t % 2) * 256
                        blkm = max(t - 1, 0)
                        blkp = min(t + 1, 15)
                        for jj in range(2):
                            j = pp * 2 + jj
                            oreg = pB[0:65, ocol + jj * 128:
                                      ocol + jj * 128 + 128]
                            ptb = PT0 + slot * 768 + 384 * jj
                            dep_pv = [(cA, texp), (cD, oev_tok[obank]),
                                      (cA, vevac[blkp]), (cA, vevac[blkm]),
                                      (cA, vevac[t])]
                            ph.op("tensor", lambda eng, oreg=oreg, hg=hg, j=j,
                                  ptb=ptb: eng.matmul(
                                    oreg,
                                    spv_sb[:, hg * 260 + 65 * j:
                                           hg * 260 + 65 * j + 65],
                                    A2[0:64, ptb:ptb + 128],
                                    start=True, stop=False,
                                    tile_position=(0, 0)), inc=cT,
                                  deps=dep_pv)
                            ph.op("tensor", lambda eng, oreg=oreg, t=t, j=j,
                                  ptb=ptb, vext0=vext0: eng.matmul(
                                    oreg,
                                    A2[:, vext0 + t * 260 + 65 * j:
                                       vext0 + t * 260 + 65 * j + 65],
                                    A2[:, ptb + 128:ptb + 256],
                                    start=False, stop=False), inc=cT)
                            ph.op("tensor", lambda eng, oreg=oreg, blkm=blkm,
                                  j=j, ptb=ptb, vext0=vext0: eng.matmul(
                                    oreg,
                                    A2[64:128, vext0 + blkm * 260 + 65 * j:
                                       vext0 + blkm * 260 + 65 * j + 65],
                                    A2[64:128, ptb:ptb + 128],
                                    start=False, stop=False,
                                    tile_position=(64, 0)), inc=cT)
                            pv_tok[slot] = ph.op(
                                "tensor", lambda eng, oreg=oreg, blkp=blkp,
                                j=j, ptb=ptb, vext0=vext0: eng.matmul(
                                    oreg,
                                    A2[:, vext0 + blkp * 260 + 65 * j:
                                       vext0 + blkp * 260 + 65 * j + 65],
                                    A2[:, ptb + 256:ptb + 384],
                                    start=False, stop=True), inc=cT)
                        if t % 2 == 1:
                            t0 = t - 1
                            depv = [(cT, pv_tok[slot]), (dOw, last_oscr)]
                            tev = ph.op(
                                "vector", lambda eng, obank=obank, t0=t0:
                                eng.tensor_copy(
                                    out=opair.rearrange(
                                        "p (j c) -> p j c", j=2)[
                                        :, :, t0 * 128:t0 * 128 + 256]
                                    .rearrange(
                                        "p j (tt c) -> p j tt c", tt=2),
                                    in_=pB[0:65, obank * 512:
                                           (obank + 1) * 512].rearrange(
                                        "p (tt j c) -> p j tt c",
                                        tt=2, j=2)), inc=cD, deps=depv)
                            oev_tok[obank] = tev
                    for jj in range(2):
                        j = pp * 2 + jj
                        last_oscr = dma(ph, "dOw",
                                        o_scr_h[s, hg * 4 + j],
                                        opair[0:64, jj * 2048:(jj + 1) * 2048],
                                        deps=[(cD, oev_tok[0]),
                                              (cD, oev_tok[1])])
                    tn1 = dma(ph, "dNorm", nsum_h[pp].unsqueeze(0),
                              opair[64:65, 0:4096],
                              deps=[(cD, oev_tok[0]), (cD, oev_tok[1])],
                              serial=True)
                    tn2 = dma(ph, "dNorm", rc_in[:, 0:32],
                              nsum_h[pp].rearrange("(p c) -> p c", p=128),
                              deps=[(dNorm, tn1), (cD, cD.n)], serial=True)
                    def _recip(eng):
                        with nc.allow_low_precision(
                                reason="softmax denominators kept bf16"):
                            return eng.reciprocal(out=rc_out[:, 0:32],
                                                  in_=rc_in[:, 0:32])
                    trc = ph.op("vector", _recip, inc=cD,
                                deps=[(dNorm, tn2)])
                    tn3 = dma(ph, "dNorm",
                              rrow_h[s, hg * 4 + pp * 2:hg * 4 + pp * 2 + 2]
                              .rearrange("h t -> (h t)")
                              .rearrange("(p c) -> p c", p=128),
                              rc_out[:, 0:32], deps=[(cD, trc)], serial=True)
                att_done_ct = cT.n
                if s == 0:
                    p0_pump(ph, 70)
            if s == 0:
                p0_pump(ph, len(p0_items))
            prev_toks = bar_end(ph, NB)
            ph.build()

            if "C" not in BUILD_PHASES:
                continue
            ph = Ph(nc)
            bar_start(ph, prev_toks, NB)
            ph.wait("gpsimd", [(ECTR[e], prev_toks[e]) for e in prev_toks])
            wo_mm_prev = 0
            prev_trb = 0
            wo_last_mm = [0] * 16
            for tc in range(NTC):
                o_toks = []
                for hh in range(NH):
                    dep = [(cT, wo_mm_prev)] if hh == 0 else []
                    o_toks.append(dma(
                        ph, "dOtc",
                        A2[(hh % 2) * 64:(hh % 2) * 64 + 64,
                           OTC0 + (hh // 2) * TC: OTC0 + (hh // 2 + 1) * TC],
                        o_scr_h[s, hh][:, tc * TC:(tc + 1) * TC], deps=dep))
                RB0c = 24576
                for hh in range(NH):
                    o_toks.append(dma(
                        ph, "dOtc",
                        A2[(hh % 2) * 64:(hh % 2) * 64 + 64,
                           RB0c + (hh // 2) * TC: RB0c + (hh // 2 + 1) * TC],
                        rrow_h[s, hh][tc * TC:(tc + 1) * TC].unsqueeze(0)
                        .broadcast_to([64, TC]),
                        deps=[(cD, prev_trb)] if hh == 0 else []))
                o_all = o_toks[-1]
                prev_trb = ph.op("vector", lambda eng: eng.tensor_mul(
                    out=A2[:, OTC0:OTC0 + 8192],
                    in0=A2[:, OTC0:OTC0 + 8192],
                    in1=A2[:, RB0c:RB0c + 8192]), inc=cD,
                    deps=[(dOtc, o_all)])
                ev_toks = []
                mm_toks = []
                wol = []
                for mc in range(16):
                    wbuf = mc % 4
                    dep_w = [(cT, mm_toks[mc - 4])] if mc >= 4 else \
                        [(cT, wo_last_mm[12 + mc])]
                    if s == 0 and tc == 0 and mc < 4:
                        dep_w += p0_sdep(191)
                    twl = dma(ph, dWOS[wbuf],
                              A2[:, WOR0 + wbuf * 2048:
                                 WOR0 + (wbuf + 1) * 2048],
                              wo_scr_h[mc], deps=dep_w)
                    wol.append(twl)
                    ps = pA if mc % 2 == 0 else pB
                    tmm = 0
                    for kc in range(NKC):
                        dep = [(dWos[wbuf], twl), (cD, prev_trb)]
                        if mc >= 2 and kc == 0:
                            dep.append((cA, ev_toks[mc - 2]))
                        tmm = ph.op("tensor", lambda eng, ps=ps, wbuf=wbuf,
                                    kc=kc: eng.matmul(
                                ps[:, 0:512],
                                A2[:, WOR0 + wbuf * 2048 + kc * 128:
                                   WOR0 + wbuf * 2048 + (kc + 1) * 128],
                                A2[:, OTC0 + kc * TC: OTC0 + (kc + 1) * TC],
                                start=(kc == 0), stop=(kc == NKC - 1)),
                                inc=cT, deps=dep)
                    mm_toks.append(tmm)
                    ev_toks.append(ph.op("scalar", lambda eng, ps=ps, mc=mc:
                                         eng.activation(
                            out=A2[:, XBFC0 + mc * TC: XBFC0 + (mc + 1) * TC],
                            in_=ps[:, 0:512], func=AF.Identity,
                            bias=bo_sb[:, mc:mc + 1]), inc=cA,
                            deps=[(cT, tmm)]))
                wo_mm_prev = cT.n
                wo_last_mm = mm_toks
                tw = ln_stats(ph, s, 1, tc,
                              lambda kc: A2[:, XBFC0 + kc * TC:
                                            XBFC0 + (kc + 1) * TC],
                              lambda kc: [(cA, ev_toks[kc])], False,
                              sev=s1evacC)
                tb1 = bcast(ph, mnr_b[:], s, 1, tc, 0,
                            deps=[(dStat, tw), (cD, cD.n)], n=2)
                tb2 = tb1
                x1a = []
                for kc in range(NKC):
                    dep1 = [(dStat, tb1)]
                    if kc == 0:
                        dep1 += [(dAcc, dAcc.n), (dX1w, dX1w.n)]
                    ph.op("vector", lambda eng, kc=kc: eng.tensor_add(
                        out=u_sb[:, kc * TC:(kc + 1) * TC],
                        in0=A2[:, XBFC0 + kc * TC: XBFC0 + (kc + 1) * TC],
                        in1=mneg_b[:]), inc=cD, deps=dep1)
                    t2 = ph.op("vector", lambda eng, kc=kc: eng.tensor_mul(
                        out=u_sb[:, kc * TC:(kc + 1) * TC],
                        in0=u_sb[:, kc * TC:(kc + 1) * TC], in1=r_b[:]),
                        inc=cD, deps=[(dStat, tb2)])
                    x1a.append(dma(ph, "dAcc",
                                   u_sb[:, kc * TC:(kc + 1) * TC],
                                   u_t[s, kc * 128:(kc + 1) * 128,
                                       tc * TC:(tc + 1) * TC],
                                   deps=[(cD, t2)], accum=True))
                cp2 = []
                for kc in range(NKC):
                    cp2.append(ph.op("scalar", lambda eng, kc=kc:
                                     eng.activation(
                            out=A1[:, kc * TC:(kc + 1) * TC],
                            in_=u_sb[:, kc * TC:(kc + 1) * TC], func=AF.Copy),
                            inc=cA, deps=[(dAcc, x1a[-1])]))
                    dma(ph, "dX1w",
                        x1_scr_h[s, kc][:, tc * TC:(tc + 1) * TC],
                        u_sb[:, kc * TC:(kc + 1) * TC],
                        deps=[(dAcc, x1a[-1])])
                x1stat[(s, tc)] = ln_stats(
                    ph, s, 2, tc,
                    lambda kc: A1[:, kc * TC:(kc + 1) * TC],
                    lambda kc: [(cA, cp2[kc])], False, sev=s1evacC)
            prev_toks = bar_end(ph, ENGINES)
            ph.build()

            if "D" not in BUILD_PHASES:
                continue
            ph = Ph(nc)
            bar_start(ph, prev_toks, ENGINES)
            h2_prev = 0
            e_prev = 0
            x2_last_cd = 0
            x2_last_dout = 0
            pending_x2 = None
            w0_last_mm = [0] * 64
            w1_hist4 = [0] * 4
            w0_final = 0
            for tc in range(NTC):
                tb2 = bcast(ph, r_b, s, 2, tc, 1,
                            deps=[(dStat, x1stat[(s, tc)]), (cD, cD.n)])
                pre_toks = []
                for kc in range(NKC):
                    dep = [(cD, h2_prev)] if kc == 0 else []
                    pre_toks.append(dma(
                        ph, "dPre", u_sb[:, kc * TC:(kc + 1) * TC],
                        statrow_h[s, 2, tc, 0].unsqueeze(0)
                        .broadcast_to([128, TC]), deps=dep))
                acc_toks = []
                for kc in range(NKC):
                    acc_toks.append(dma(
                        ph, "dAcc", u_sb[:, kc * TC:(kc + 1) * TC],
                        x1_scr_h[s, kc][:, tc * TC:(tc + 1) * TC],
                        deps=[(dPre, pre_toks[-1])], accum=True))
                h2_toks = []
                for kc in range(NKC):
                    h2_toks.append(ph.op("vector", lambda eng, kc=kc:
                                         eng.tensor_mul(
                            out=A2[:, H20 + kc * TC: H20 + (kc + 1) * TC],
                            in0=u_sb[:, kc * TC:(kc + 1) * TC], in1=r_b[:]),
                            inc=cD, deps=[(dAcc, acc_toks[-1]),
                                          (dStat, tb2), (cT, w0_final)]))
                h2_prev = h2_toks[-1]
                if pending_x2 is not None:
                    x2_last_dout, x2_last_cd = pending_x2(ph)
                    pending_x2 = None
                g_toks = []
                w0mm = []
                for mc in range(64):
                    wbuf = mc % 4
                    dep_w = [(cT, w0mm[mc - 4])] if mc >= 4 else \
                        [(cT, w0_last_mm[60 + mc])]
                    if s == 0 and tc == 0 and mc < 4:
                        dep_w += p0_sdep(447)
                    twl = dma(ph, dW0S[wbuf],
                              A2[:, W0R0 + wbuf * 2048:
                                 W0R0 + (wbuf + 1) * 2048],
                              w0_scr_h[mc], deps=dep_w)
                    ps = pA if mc % 2 == 0 else pB
                    pcol = ((mc % 4) // 2) * 512
                    tmm = 0
                    for kc in range(NKC):
                        dep = [(dW0s[wbuf], twl), (cD, h2_toks[kc])]
                        if kc == 0:
                            dep.append((cA, g_toks[mc - 4] if mc >= 4
                                        else e_prev))
                        tmm = ph.op("tensor", lambda eng, ps=ps, pcol=pcol,
                                    wbuf=wbuf, kc=kc: eng.matmul(
                                ps[:, pcol:pcol + 512],
                                A2[:, W0R0 + wbuf * 2048 + kc * 128:
                                   W0R0 + wbuf * 2048 + (kc + 1) * 128],
                                A2[:, H20 + kc * TC: H20 + (kc + 1) * TC],
                                start=(kc == 0), stop=(kc == NKC - 1)),
                                inc=cT, deps=dep)
                    w0mm.append(tmm)
                    g_toks.append(ph.op("scalar", lambda eng, ps=ps, pcol=pcol,
                                        mc=mc: eng.activation(
                            out=d_sb[:, mc * TC:(mc + 1) * TC],
                            in_=ps[:, pcol:pcol + 512], func=AF.Gelu,
                            bias=b0_sb[:, mc:mc + 1]), inc=cA,
                            deps=[(cT, tmm)]))
                w0_last_mm = w0mm
                w0_final = w0mm[-1]
                W1COL = (W1R0, W1R0 + 1024, 30720, 31744)
                e_ev = []
                for oh in range(2):
                    mm_last = []
                    for kc in range(64):
                        wbuf = kc % 4
                        dep_w = [(cT, mm_last[kc - 4])] if kc >= 4 else \
                            [(cT, w1_hist4[kc])]
                        if s == 0 and tc == 0 and oh == 0 and kc < 4:
                            dep_w += p0_sdep(703)
                        twl = dma(ph, dW1S[wbuf],
                                  A2[:, W1COL[wbuf]:W1COL[wbuf] + 1024],
                                  w1_scr_h[oh, kc], deps=dep_w)
                        tmm = 0
                        for oc in range(8):
                            ps = pA if oc < 4 else pB
                            pcol = (oc % 4) * 512
                            dep = [(dW1s[wbuf], twl), (cA, g_toks[kc])]
                            if kc == 0:
                                dep.append((cA, g_toks[-1] if oh == 0
                                            else e_ev[-1]))
                                dep.append((cD, state["stats_ev_cd"]))
                            tmm = ph.op("tensor", lambda eng, ps=ps, pcol=pcol,
                                        wbuf=wbuf, oc=oc, kc=kc: eng.matmul(
                                    ps[:, pcol:pcol + 512],
                                    A2[:, W1COL[wbuf] + oc * 128:
                                       W1COL[wbuf] + (oc + 1) * 128],
                                    d_sb[:, kc * TC:(kc + 1) * TC],
                                    start=(kc == 0), stop=(kc == 63)),
                                    inc=cT, deps=dep)
                        mm_last.append(tmm)
                    w1_hist4 = mm_last[60:64]
                    for oc in range(8):
                        ps = pA if oc < 4 else pB
                        pcol = (oc % 4) * 512
                        idx = oh * 8 + oc
                        e_ev.append(ph.op("scalar", lambda eng, ps=ps,
                                          pcol=pcol, idx=idx: eng.activation(
                                out=A2[:, E0 + idx * TC: E0 + (idx + 1) * TC],
                                in_=ps[:, pcol:pcol + 512], func=AF.Identity,
                                bias=b1_sb[:, idx:idx + 1]), inc=cA,
                                deps=[(cT, mm_last[-1]),
                                      (cD, x2_last_cd),
                                      (dOut, x2_last_dout)]))

                tw3 = ln_stats(ph, s, 3, tc,
                               lambda kc: A2[:, E0 + kc * TC:
                                             E0 + (kc + 1) * TC],
                               lambda kc: [(cA, e_ev[-1] if kc == 0
                                            else e_ev[kc])], True,
                               sev=s1evacD)
                tb31 = bcast(ph, r3_b[:], s, 3, tc, 1,
                             deps=[(dStat, tw3), (cD, cD.n)])

                def emit_x2(ph, tc=tc, tb31=tb31):
                    y3 = lambda par: A2f[:, (Y30 // 2) + par * 512:
                                         (Y30 // 2) + (par + 1) * 512]
                    x2t = lambda par: A2f[:, (X20 // 2) + par * 512:
                                          (X20 // 2) + (par + 1) * 512]
                    x2tok = [0, 0]
                    touts = []
                    for kc in range(NKC):
                        par = kc % 2
                        dep_z = [(dStat, tb31)]
                        if kc >= 2:
                            dep_z.append((cD, x2tok[par]))
                            dep_z.append((dOut, touts[kc - 2]))
                        tpz = dma(ph, "dPreX", x2t(par),
                                  statrow_h[s, 3, tc, 2].unsqueeze(0)
                                  .broadcast_to([128, TC]), deps=dep_z,
                                  serial=True)
                        taz = dma(ph, "dAccX", x2t(par),
                                  x1_scr_h[s, kc][:, tc * TC:(tc + 1) * TC],
                                  deps=[(dPreX, tpz)], accum=True,
                                  serial=True)
                        ph.op("vector", lambda eng, kc=kc, par=par:
                              eng.tensor_mul(
                                out=y3(par),
                                in0=A2[:, E0 + kc * TC: E0 + (kc + 1) * TC],
                                in1=r3_b[:]), inc=cD, deps=[(dStat, tb31)])
                        tx = ph.op("vector", lambda eng, par=par:
                                   eng.tensor_add(
                                out=x2t(par), in0=x2t(par), in1=y3(par)),
                                inc=cD, deps=[(dAccX, taz)])
                        x2tok[par] = tx
                        touts.append(dma(ph, "dOut",
                            out_t[s, kc * 128:(kc + 1) * 128,
                                  tc * TC:(tc + 1) * TC],
                            x2t(par), deps=[(cD, tx)], serial=True))
                    return touts[-1], cD.n
                pending_x2 = emit_x2
                e_prev = e_ev[-1]
            if pending_x2 is not None:
                x2_last_dout, x2_last_cd = pending_x2(ph)
                pending_x2 = None
            prev_toks = bar_end(ph, ENGINES)
            ph.build()

        ph = Ph(nc)
        bar_start(ph, prev_toks, ENGINES)
        if KDBG:
            dma(ph, "dTbl", dbg_o, o_scr_h.ap())
            dma(ph, "dTbl", dbg_x1, x1_scr_h.ap())
            dma(ph, "dTbl", dbg_st, statrow_h.ap())
            dma(ph, "dTbl", dbg_ns, nsum_h.ap())
        def _final_waits(eng):
            for nm, (dc, dq) in DSEMS.items():
                eng.wait_ge(dc.h, dc.n)
        ph.raw("sync", _final_waits)
        ph.build()

    return nc


def kernel(L0_attn_u, L0_attn_Wqkv, L0_attn_bqkv, L0_attn_Wo, L0_attn_bo,
           L0_attn_sinks, L0_ffn_W0, L0_ffn_bias0, L0_ffn_W1, L0_ffn_bias1):
    u_t, shared = host_prep(
        np.asarray(L0_attn_u, dtype=np.float32),
        np.asarray(L0_attn_Wqkv, dtype=np.float32),
        np.asarray(L0_attn_bqkv, dtype=np.float32),
        np.asarray(L0_attn_Wo, dtype=np.float32),
        np.asarray(L0_attn_bo, dtype=np.float32),
        np.asarray(L0_attn_sinks, dtype=np.float32),
        np.asarray(L0_ffn_W0, dtype=np.float32),
        np.asarray(L0_ffn_bias0, dtype=np.float32),
        np.asarray(L0_ffn_W1, dtype=np.float32),
        np.asarray(L0_ffn_bias1, dtype=np.float32))
    nc = build_nc()
    in_maps = []
    for c in range(NCORES):
        m = dict(shared)
        m["u_t"] = np.ascontiguousarray(u_t[c * NSEQ:(c + 1) * NSEQ])
        in_maps.append(m)
    res = run_bass_kernel_spmd(nc, in_maps, core_ids=list(range(NCORES)))
    outs = [r["out_t"] for r in res.results]
    out = np.concatenate(outs, axis=0).transpose(0, 2, 1)
    return np.ascontiguousarray(out).astype(np.float32)


if __name__ == "__main__":
    nc = build_nc()
    n = sum(1 for _ in nc.all_instructions())
    print("built ok,", n, "instructions")
